# version 14
# baseline (speedup 1.0000x reference)
import numpy as np
from contextlib import ExitStack

import concourse.bass as bass
import concourse.mybir as mybir
from concourse.bass_utils import run_bass_kernel_spmd

F32 = mybir.dt.float32
BF16 = mybir.dt.bfloat16
AF = mybir.ActivationFunctionType
ALU = mybir.AluOpType
AX = mybir.AxisListType

HALO, TO, TH, C = 256, 1024, 1280, 2048
NTG = 10
NOG = 8
G = 16
NE, NI, NIG = 8, 1024, 8
RING = 12
FFN_NE = 8
FFN_PART = 5
DBG = False
FINAL_POOL = False
EPS = 1e-5
NEG = -3.0e38

ENGS = ("tensor", "vector", "scalar", "gpsimd", "sync")


class Plan:
    def __init__(self):
        self.items = {e: [] for e in ENGS}
        self.cnt = {}
        self.S = {}

    def cur(self, s):
        return self.cnt.get(s, 0)

    def bump(self, s, a):
        self.cnt[s] = self.cnt.get(s, 0) + a
        return self.cnt[s]

    def op(self, eng, fn, wait=(), inc=()):
        for s, a in inc:
            self.bump(s, a)
        wait = tuple((s, v) for (s, v) in wait if v > 0)
        inc = tuple(inc)

        def run(E, fn=fn, wait=wait, inc=inc, S=self.S):
            for s, v in wait:
                E.wait_ge(S[s], v)
            ins = fn(E)
            for s, a in inc:
                ins.then_inc(S[s], a)
        self.items[eng].append(run)

    def wait(self, eng, s, v):
        if v > 0:
            self.items[eng].append(lambda E, s=s, v=v, S=self.S: E.wait_ge(S[s], v))

    def svop(self, eng, fn, wait=()):
        sem = "sv_" + eng
        self.op(eng, fn, wait=wait, inc=[(sem, 1)])
        self.wait(eng, sem, self.cur(sem))


PHASES = ("setup", "ln", "tp", "cls", "route", "attn", "wout", "ffn", "final")


def build_program(upto="final"):
    lim = PHASES.index(upto)

    def cut(name):
        return PHASES.index(name) > lim
    nc = bass.Bass("TRN2", target_bir_lowering=False)
    p = Plan()

    x_in = nc.dram_tensor("x", [TH, C], F32, kind="ExternalInput")
    wrvk = nc.dram_tensor("wrvk", [G, 3, 128, 2048], F32, kind="ExternalInput")
    wout_d = nc.dram_tensor("wout", [16, 128, 2048], F32, kind="ExternalInput")
    wcls_d = nc.dram_tensor("wcls", [128, 192], F32, kind="ExternalInput")
    wgu = nc.dram_tensor("wgu", [NE, 16, 128, 2048], F32, kind="ExternalInput")
    wdn = nc.dram_tensor("wdn", [NE, 8, 128, 2048], F32, kind="ExternalInput")
    td_d = nc.dram_tensor("td", [128, 16], F32, kind="ExternalInput")
    tf_d = nc.dram_tensor("tf", [128, 16], F32, kind="ExternalInput")
    brep_d = nc.dram_tensor("brep", [128, 12], F32, kind="ExternalInput")
    ident_d = nc.dram_tensor("ident", [128, 128], F32, kind="ExternalInput")
    mask_d = nc.dram_tensor("mask", [128, HALO], F32, kind="ExternalInput")
    out_d = nc.dram_tensor("out", [TO, C], F32, kind="ExternalOutput")
    if DBG:
        dbg_ar = nc.dram_tensor("dbg_ar", [128, 9472], F32, kind="ExternalOutput")
        dbg_y = nc.dram_tensor("dbg_y", [128, 16384], BF16, kind="ExternalOutput")
        dbg_xn = nc.dram_tensor("dbg_xn", [128, G * TH], BF16, kind="ExternalOutput")
        dbg_mr = nc.dram_tensor("dbg_mr", [128, 2 * NTG], F32, kind="ExternalOutput")
        dbg_rt = nc.dram_tensor("dbg_rt", [128, 96], F32, kind="ExternalOutput")
        dbg_acc = nc.dram_tensor("dbg_acc", [128, 16384], F32,
                                 kind="ExternalOutput")
        dbg_sk = nc.dram_tensor("dbg_sk", [128, NOG], F32, kind="ExternalOutput")
        dbg_rb = nc.dram_tensor("dbg_rb", [128, 384], F32, kind="ExternalOutput")
        dbg_lg = nc.dram_tensor("dbg_lg", [12, 1024], F32, kind="ExternalOutput")
        dbg_pr = nc.dram_tensor("dbg_pr", [128, 96], F32, kind="ExternalOutput")
        dbg_act = nc.dram_tensor("dbg_act", [128, 8192], BF16, kind="ExternalOutput")
        dbg_w8b = nc.dram_tensor("dbg_w8b", [128, 3072], F32, kind="ExternalOutput")
    schi = nc.dram_tensor("schi", [TH, C], BF16)
    w8dram = nc.dram_tensor("w8dram", [12, 1024], F32)
    sclo = nc.dram_tensor("sclo", [TO, C], BF16)

    ctx = ExitStack()
    E = ctx.enter_context

    ident = E(nc.sbuf_tensor("identsb", [128, 128], F32))
    td = E(nc.sbuf_tensor("tdsb", [128, 16], F32))
    tf = E(nc.sbuf_tensor("tfsb", [128, 16], F32))
    lam = E(nc.sbuf_tensor("lamsb", [128, 16], F32))
    eu = E(nc.sbuf_tensor("eusb", [128, 16], F32))
    brep = E(nc.sbuf_tensor("brepsb", [128, 12], F32))
    mask = E(nc.sbuf_tensor("masksb", [128, HALO], F32))
    wcls = E(nc.sbuf_tensor("wclssb", [128, 192], F32))
    clsh = E(nc.sbuf_tensor("clsh", [128, 192], BF16))
    clsl = E(nc.sbuf_tensor("clsl", [128, 192], BF16))
    ones = E(nc.sbuf_tensor("onesrow", [1, 128], F32))
    sumc = E(nc.sbuf_tensor("sumc", [128, NTG], F32))
    sqc = E(nc.sbuf_tensor("sqc", [128, NTG], F32))
    meanc = E(nc.sbuf_tensor("meanc", [128, NTG], F32))
    rstdc = E(nc.sbuf_tensor("rstdc", [128, NTG], F32))
    skipc = E(nc.sbuf_tensor("skipc", [128, NOG], F32))
    tmpa = E(nc.sbuf_tensor("tmpa", [128, 1], F32))
    tmpb = E(nc.sbuf_tensor("tmpb", [128, 1], F32))
    tmpc = E(nc.sbuf_tensor("tmpc", [128, 1], F32))
    epsc = E(nc.sbuf_tensor("epsc", [128, 1], F32))
    lgb = E(nc.sbuf_tensor("lgb", [128, 96], F32))
    eab = E(nc.sbuf_tensor("eab", [128, 96], F32))
    prb = E(nc.sbuf_tensor("prb", [128, 96], F32))
    w12 = E(nc.sbuf_tensor("w12", [128, 96], F32))
    mxc = E(nc.sbuf_tensor("mxc", [128, 8], F32))
    inv8 = E(nc.sbuf_tensor("inv8", [128, 8], F32))
    afin = E(nc.sbuf_tensor("afin", [128, 8], F32))
    bfin = E(nc.sbuf_tensor("bfin", [128, 8], F32))
    xnfm = E(nc.sbuf_tensor("xnfm", [128, G * TH], BF16))
    WS = [E(nc.sbuf_tensor(f"ws{i}", [128, 2048], BF16)) for i in range(RING)]
    AR = E(nc.sbuf_tensor("ar32", [128, 20480], F32))
    AB = E(nc.sbuf_tensor("arb16", [128, 16384], BF16))
    PS = [E(nc.psum_tensor(f"ps{i}", [128, 512], F32)) for i in range(8)]

    def ar(a, b):
        return AR[:, a:b]
    ACCo = 0
    XT = [(0, 2048), (2048, 4096)]
    XNF = [(6144, 8192), (8192, 10240)]
    LOG = 16384
    W8F = 19456
    EKo, Vo, Ao, Po, Qo = 0, 1280, 2560, 3840, 5120
    SRo, NUMo, DENo = 6400, 7424, 8448
    EK2o, V2o, SR2o = 9472, 10752, 12032
    SIL = [(16384, 16896), (16896, 17408)]
    W8B = [(17408, 18432), (18432, 19456)]
    XTF = [(16384, 18432), (18432, 20480)]

    def acc_ap(tg, a, b):
        return AR[:, 2048 * tg + a:2048 * tg + b]

    def xnh_ap(s):
        return AB[:, 2048 * s:2048 * s + 2048]

    def xnl_ap(s):
        return AB[:, 4096 + 2048 * s:4096 + 2048 * s + 2048]

    def xlr_ap(s, a, b):
        return AB[:, 1024 * s + a:1024 * s + b]

    def y_ap(kc, a, b):
        return AB[:, 1024 * kc + a:1024 * kc + b]

    def act_ap(ig, a, b):
        return AB[:, 1024 * ig + a:1024 * ig + b]

    def xnfm_ap(kc, a, b):
        return xnfm[:, TH * kc + a:TH * kc + b]

    sem_names = (["ld", "sx", "ssc", "stp", "sxl", "sout", "wfree",
                  "pe", "act", "dve", "pool", "w8b", "sv_vector", "sv_scalar"]
                 + [f"ws{i}" for i in range(RING)])
    for s in sem_names:
        p.S[s] = E(nc.semaphore(s))

    bank_free = [None] * 8

    def pe_use_banks(banks):
        w = []
        for b in banks:
            if bank_free[b] is not None:
                w.append(bank_free[b])
                bank_free[b] = None
        best = {}
        for s, v in w:
            best[s] = max(best.get(s, 0), v)
        return [(s, v) for s, v in best.items()]

    def banks_freed_by(banks, sem, val):
        for b in banks:
            bank_free[b] = (sem, val)

    slab_srcs = []
    for g in range(G):
        for s in range(3):
            slab_srcs.append(wrvk[g, s])
    for kc in range(16):
        slab_srcs.append(wout_d[kc])
    for e in range(NE):
        for kc in range(16):
            slab_srcs.append(wgu[e, kc])
        for kc in range(8):
            slab_srcs.append(wdn[e, kc])
    assert len(slab_srcs) == 256

    def emit_weight_head():
        p.wait("gpsimd", "ld", 96)
        for j in range(min(len(slab_srcs), RING)):
            p.op("gpsimd",
                 lambda E, d=WS[j % RING], s=slab_srcs[j]:
                     E.dma_start(d[:, :], s[:, :]),
                 inc=[(f"ws{j % RING}", 16)])

    ws_done = [False]

    def emit_weight_stream():
        if ws_done[0]:
            return
        ws_done[0] = True
        nfree = p.cur("wfree")
        nslab = min(len(slab_srcs), nfree + RING)
        for j in range(RING, nslab):
            p.wait("gpsimd", "wfree", j - RING + 1)
            p.op("gpsimd",
                 lambda E, d=WS[j % RING], s=slab_srcs[j]:
                     E.dma_start(d[:, :], s[:, :]),
                 inc=[(f"ws{j % RING}", 16)])

    def slab_ready_wait(j):
        return (f"ws{j % RING}", 16 * (j // RING + 1))

    for dst, src in ((ident, ident_d), (td, td_d), (tf, tf_d),
                     (brep, brep_d), (mask, mask_d), (wcls, wcls_d)):
        p.op("sync", lambda E, d=dst, s=src: E.dma_start(d[:, :], s[:, :]),
             inc=[("ld", 16)])

    p.svop("scalar", lambda E: E.activation(lam[:, :], td[:, :], AF.Exp),
           wait=[("ld", 96)])
    p.op("scalar", lambda E: E.activation(lam[:, :], lam[:, :], AF.Exp, scale=-1.0))
    p.op("scalar", lambda E: E.activation(eu[:, :], tf[:, :], AF.Exp))
    p.op("scalar", lambda E: E.activation(clsh[:, :], wcls[:, :], AF.Copy),
         inc=[("act", 1)])
    act_setup = p.cur("act")
    p.op("vector", lambda E: E.memset(ones[:, :], 1.0))
    p.op("vector", lambda E: E.memset(epsc[:, :], EPS))
    p.op("vector",
         lambda E: E.tensor_tensor(clsl[:, :], wcls[:, :], clsh[:, :], ALU.subtract),
         wait=[("act", act_setup)], inc=[("dve", 1)])
    dve_setup = p.cur("dve")

    emit_weight_head()
    if cut("ln"):
        return finish(nc, p, ctx, emit_weight_stream)
    xt_free = [0, 0]
    xnh_wr = [0, 0]
    xnl_wr = [0, 0]
    a3 = [0, 0]
    d3 = [0, 0]

    for tg in range(NTG):
        sl = tg % 2
        p.op("sync",
             lambda E, d=ar(*XT[sl]), s=x_in[128 * tg:128 * tg + 128, :]:
                 E.dma_start(d, s),
             wait=[("dve", xt_free[sl])], inc=[("sx", 16)])
        p.op("scalar",
             lambda E, x=ar(*XT[sl]), o=ar(*XNF[sl]), a=sumc[:, tg:tg + 1]:
                 E.activation(o, x, AF.Copy, accum_out=a),
             wait=[("sx", p.cur("sx")), ("dve", d3[sl])])
        p.op("scalar",
             lambda E, x=ar(*XT[sl]), o=ar(*XNF[sl]), a=sqc[:, tg:tg + 1]:
                 E.activation(o, x, AF.Square, accum_out=a),
             inc=[("act", 1)])
        a1 = p.cur("act")
        p.svop("vector",
               lambda E, o=meanc[:, tg:tg + 1], i=sumc[:, tg:tg + 1]:
                   E.tensor_scalar(o, i, 1.0 / C, None, ALU.mult),
               wait=[("act", a1)])
        p.svop("vector",
               lambda E, i=sqc[:, tg:tg + 1]:
                   E.tensor_scalar(tmpa[:, :], i, 1.0 / C, None, ALU.mult))
        p.svop("vector",
               lambda E, m=meanc[:, tg:tg + 1]:
                   E.tensor_tensor(tmpb[:, :], m, m, ALU.mult))
        p.op("vector",
             lambda E: E.tensor_tensor(tmpc[:, :], tmpa[:, :], tmpb[:, :],
                                       ALU.subtract),
             inc=[("dve", 1)])
        d1 = p.cur("dve")
        p.svop("scalar",
               lambda E: E.activation(tmpc[:, :], tmpc[:, :], AF.Ln,
                                      bias=epsc[:, 0:1]),
               wait=[("dve", d1)])
        p.op("scalar",
             lambda E, o=rstdc[:, tg:tg + 1]:
                 E.activation(o, tmpc[:, :], AF.Exp, scale=-0.5),
             inc=[("act", 1)])
        a2 = p.cur("act")
        p.op("vector",
             lambda E, o=ar(*XNF[sl]), x=ar(*XT[sl]), m=meanc[:, tg:tg + 1],
                    r=rstdc[:, tg:tg + 1]:
                 E.tensor_scalar(o, x, m, r, ALU.subtract, ALU.mult),
             wait=[("act", a2)],
             inc=[("dve", 1)])
        d2 = p.cur("dve")
        xt_free[sl] = d2
        p.op("vector",
             lambda E, o=xnh_ap(sl), i=ar(*XNF[sl]): E.tensor_copy(o, i),
             wait=[("ssc", xnh_wr[sl])], inc=[("pool", 1)])
        a3[sl] = p.cur("pool")
        if tg >= 2:
            p.op("vector",
                 lambda E, o=xnl_ap(sl), i=ar(*XNF[sl]), h=xnh_ap(sl):
                     E.tensor_tensor(o, i, h, ALU.subtract),
                 wait=[("pool", a3[sl]), ("ssc", xnl_wr[sl])], inc=[("dve", 1)])
            d3[sl] = p.cur("dve")
        p.op("gpsimd",
             lambda E, d=schi[128 * tg:128 * tg + 128, :], s=xnh_ap(sl):
                 E.dma_start(d, s),
             wait=[("pool", a3[sl])], inc=[("ssc", 16)])
        xnh_wr[sl] = p.cur("ssc")
        if tg >= 2:
            p.op("gpsimd",
                 lambda E, d=sclo[128 * (tg - 2):128 * (tg - 2) + 128, :],
                        s=xnl_ap(sl):
                     E.dma_start(d, s),
                 wait=[("dve", d3[sl])], inc=[("ssc", 16)])
            xnl_wr[sl] = p.cur("ssc")

    if cut("tp"):
        return finish(nc, p, ctx, emit_weight_stream)
    p.wait("sync", "ssc", p.cur("ssc"))
    for g in range(G):
        p.op("sync",
             lambda E, d=xnfm_ap(g, 0, TH), s=schi[0:TH, 128 * g:128 * g + 128]:
                 E.dma_start(d, s, transpose=True),
             inc=[("stp", 16)])
    stp_all = p.cur("stp")

    xlr_free = [0, 0, 0, 0]

    if cut("cls"):
        return finish(nc, p, ctx, emit_weight_stream)
    cls_banks = [0, 1]
    w0 = pe_use_banks(cls_banks)
    for g in range(G):
        p.op("sync",
             lambda E, d=xlr_ap(g % 4, 0, 1024),
                    s=sclo[0:TO, 128 * g:128 * g + 128]:
                 E.dma_start(d, s, transpose=True),
             wait=[("pe", xlr_free[g % 4])], inc=[("sxl", 16)])
        waits = [("sxl", p.cur("sxl"))]
        if g == 0:
            waits += [("stp", stp_all), ("act", act_setup), ("dve", dve_setup)]
            waits += w0
        for tc in range(2):
            ps = PS[cls_banks[tc]]
            hi = clsh[:, 12 * g:12 * g + 12]
            lo = clsl[:, 12 * g:12 * g + 12]
            xh = xnfm_ap(g, HALO + 512 * tc, HALO + 512 * tc + 512)
            xl = xlr_ap(g % 4, 512 * tc, 512 * tc + 512)
            p.op("tensor",
                 lambda E, ps=ps, w=hi, x=xh, st=(g == 0):
                     E.matmul(ps[0:12, :], w, x, start=st, stop=False),
                 wait=waits if tc == 0 else ())
            p.op("tensor",
                 lambda E, ps=ps, w=lo, x=xh:
                     E.matmul(ps[0:12, :], w, x, start=False, stop=False))
            inc = [("pe", 1)] if tc == 1 else []
            p.op("tensor",
                 lambda E, ps=ps, w=hi, x=xl, sp=(g == G - 1):
                     E.matmul(ps[0:12, :], w, x, start=False, stop=sp),
                 inc=inc)
        xlr_free[g % 4] = p.cur("pe")
    cls_done = p.cur("pe")
    p.op("scalar",
         lambda E, o=AR[0:12, LOG:LOG + 512]:
             E.activation(o, PS[0][0:12, :], AF.Copy),
         wait=[("pe", cls_done)])
    p.op("scalar",
         lambda E, o=AR[0:12, LOG + 512:LOG + 1024]:
             E.activation(o, PS[1][0:12, :], AF.Copy),
         inc=[("act", 1)])
    lg_evac = p.cur("act")
    banks_freed_by([0, 1], "act", lg_evac)

    w1 = pe_use_banks([2])
    for tg in range(NOG):
        inc = [("pe", 1)] if tg == NOG - 1 else []
        p.op("tensor",
             lambda E, o=PS[2][:, 12 * tg:12 * tg + 12],
                    i=AR[0:12, LOG + 128 * tg:LOG + 128 * tg + 128]:
                 E.transpose(o, i, ident[0:12, 0:12]),
             wait=(w1 + [("act", lg_evac)]) if tg == 0 else (), inc=inc)
    lgT_done = p.cur("pe")
    p.op("vector",
         lambda E: E.tensor_copy(lgb[:, :], PS[2][:, 0:96]),
         wait=[("pe", lgT_done)], inc=[("dve", 1)])
    banks_freed_by([2], "dve", p.cur("dve"))

    def b3(buf):
        return buf[:, :].rearrange("p (b t) -> p b t", t=12)

    def bc8(col8):
        return col8[:, :].unsqueeze(2).broadcast_to((128, 8, 12))

    p.svop("vector", lambda E: E.reduce_max(mxc[:, :], b3(lgb), axis=AX.X))
    p.op("vector",
         lambda E: E.tensor_scalar(mxc[:, :], mxc[:, :], -1.0, None, ALU.mult),
         inc=[("dve", 1)])
    d_mx = p.cur("dve")
    for tg in range(NOG):
        s = slice(12 * tg, 12 * tg + 12)
        inc = [("act", 1)] if tg == NOG - 1 else []
        p.op("scalar",
             lambda E, o=eab[:, s], i=lgb[:, s], b=mxc[:, tg:tg + 1]:
                 E.activation(o, i, AF.Exp, bias=b),
             wait=[("dve", d_mx)] if tg == 0 else (), inc=inc)
    a_exp = p.cur("act")
    p.svop("vector", lambda E: E.reduce_sum(mxc[:, :], b3(eab), axis=AX.X),
           wait=[("act", a_exp)])
    p.svop("vector", lambda E: E.reciprocal(inv8[:, :], mxc[:, :]))
    p.op("vector",
         lambda E: E.tensor_tensor(b3(prb), b3(eab), bc8(inv8), ALU.mult))
    p.op("vector",
         lambda E, bb=brep[:, :].unsqueeze(1).broadcast_to((128, 8, 12)):
             E.tensor_tensor(b3(lgb), b3(prb), bb, ALU.add))
    p.svop("vector", lambda E: E.reduce_max(mxc[:, :], b3(lgb), axis=AX.X))
    p.op("vector",
         lambda E: E.tensor_tensor(b3(w12), b3(lgb), bc8(mxc), ALU.is_ge))
    p.op("vector",
         lambda E: E.scalar_tensor_tensor(b3(w12), b3(w12), NEG, b3(lgb),
                                          ALU.mult, ALU.add))
    p.svop("vector", lambda E: E.reduce_max(inv8[:, :], b3(w12), axis=AX.X))
    p.op("vector",
         lambda E: E.tensor_tensor(b3(w12), b3(lgb), bc8(inv8), ALU.is_ge))
    p.op("vector",
         lambda E: E.scalar_tensor_tensor(b3(w12), b3(prb), 1.5, b3(w12),
                                          ALU.mult, ALU.mult))
    p.svop("vector",
           lambda E, i=b3(w12)[:, :, 8:12]:
               E.reduce_sum(skipc[:, :], i, axis=AX.X))
    p.svop("vector",
           lambda E: E.tensor_tensor(afin[:, :], rstdc[:, 2:10], skipc[:, :],
                                     ALU.mult))
    p.svop("vector",
           lambda E: E.tensor_tensor(bfin[:, :], meanc[:, 2:10], afin[:, :],
                                     ALU.mult))
    p.svop("vector",
           lambda E: E.tensor_scalar(bfin[:, :], bfin[:, :], -1.0, None,
                                     ALU.mult))
    p.svop("vector",
           lambda E: E.tensor_scalar(afin[:, :], afin[:, :], 1.0, None,
                                     ALU.add))
    p.op("vector", lambda E: E.tensor_copy(tmpa[:, :], tmpa[:, :]),
         inc=[("dve", 1)])
    d_rt = p.cur("dve")

    if cut("attn"):
        return finish(nc, p, ctx, emit_weight_stream)
    FULL_CH = [(0, 512), (512, 1024), (1024, 1280)]
    OWN_CH = [(HALO, HALO + 512), (HALO + 512, TH)]
    lamap = {g: lam[:, g:g + 1].broadcast_to((128, TH)) for g in range(G)}
    wkv_done = 0
    wkv_dones = []
    sub_banks = {"r": [0, 1], "v": [2, 3, 4], "k": [5, 6, 7]}

    for g in range(G):
        eko = EKo if g % 2 == 0 else EK2o
        vo = Vo if g % 2 == 0 else V2o
        sro = SRo if g % 2 == 0 else SR2o
        evac_acts = {}
        for si, (sname, chunks) in enumerate(
                (("r", OWN_CH), ("v", FULL_CH), ("k", FULL_CH))):
            j = 3 * g + si
            banks = sub_banks[sname]
            waits = [slab_ready_wait(j)] + pe_use_banks(banks)
            if g == 0 and si == 0:
                waits.append(("stp", stp_all))
            wslab = WS[j % RING]
            nkc = 16
            for kc in range(nkc):
                wt = wslab[:, 128 * kc:128 * kc + 128]
                for ci, (c0, c1) in enumerate(chunks):
                    last = (kc == nkc - 1 and ci == len(chunks) - 1)
                    inc = ([("pe", 1)] if last else [])
                    p.op("tensor",
                         lambda E, ps=PS[banks[ci]], w=wt,
                                x=xnfm_ap(kc, c0, c1), st=(kc == 0),
                                sp=(kc == nkc - 1), n=c1 - c0:
                             E.matmul(ps[:, 0:n], w, x, start=st, stop=sp),
                         wait=waits if kc == 0 and ci == 0 else (), inc=inc)
            evac_acts[sname] = p.cur("pe")

        war = [("dve", wkv_dones[g - 2] if g >= 2 else 0)]
        p.op("scalar",
             lambda E, o=ar(sro, sro + 512), i=PS[0][:, :]:
                 E.activation(o, i, AF.Sigmoid),
             wait=[("pe", evac_acts["r"])] + war, inc=[("wfree", 1)])
        p.op("scalar",
             lambda E, o=ar(sro + 512, sro + 1024), i=PS[1][:, :]:
                 E.activation(o, i, AF.Sigmoid),
             inc=[("act", 1)])
        banks_freed_by([0, 1], "act", p.cur("act"))
        p.wait("scalar", "pe", evac_acts["v"])
        for ci, (c0, c1) in enumerate(FULL_CH):
            inc = [("wfree", 1)] if ci == 0 else ([("act", 1)] if ci == 2 else [])
            p.op("scalar",
                 lambda E, o=ar(vo + c0, vo + c1), i=PS[2 + ci][:, 0:c1 - c0]:
                     E.activation(o, i, AF.Copy),
                 inc=inc)
        banks_freed_by([2, 3, 4], "act", p.cur("act"))
        p.wait("scalar", "pe", evac_acts["k"])
        for ci, (c0, c1) in enumerate(FULL_CH):
            inc = [("wfree", 1)] if ci == 0 else ([("act", 1)] if ci == 2 else [])
            p.op("scalar",
                 lambda E, o=ar(eko + c0, eko + c1), i=PS[5 + ci][:, 0:c1 - c0]:
                     E.activation(o, i, AF.Exp),
                 inc=inc)
        banks_freed_by([5, 6, 7], "act", p.cur("act"))
        a_ev = p.cur("act")

        if g == G - 1:
            w2 = pe_use_banks([3, 4])
            for tg8 in range(NOG):
                b = 3 + tg8 // 4
                inc = [("pe", 1)] if tg8 == NOG - 1 else []
                p.op("tensor",
                     lambda E, o=PS[b][0:12,
                                       128 * (tg8 % 4):128 * (tg8 % 4) + 128],
                            i=w12[:, 12 * tg8:12 * tg8 + 12]:
                         E.transpose(o, i, ident[:, :]),
                     wait=(w2 + [("dve", d_rt)]) if tg8 == 0 else (), inc=inc)
            w8T_done = p.cur("pe")
            p.op("scalar",
                 lambda E, o=AR[0:12, W8F:W8F + 512]:
                     E.activation(o, PS[3][0:12, :], AF.Copy),
                 wait=[("pe", w8T_done)])
            p.op("scalar",
                 lambda E, o=AR[0:12, W8F + 512:W8F + 1024]:
                     E.activation(o, PS[4][0:12, :], AF.Copy),
                 inc=[("act", 1)])
            w8fm_ready = p.cur("act")
            banks_freed_by([3, 4], "act", w8fm_ready)
            p.op("sync",
                 lambda E, s=AR[0:12, W8F:W8F + 1024]:
                     E.dma_start(w8dram[:, :], s),
                 wait=[("act", w8fm_ready)], inc=[("w8b", 16)])

        p.op("vector",
             lambda E, ek=ar(eko, eko + HALO):
                 E.tensor_tensor(ek, ek, mask[:, :], ALU.mult),
             wait=[("act", a_ev)])
        p.op("vector",
             lambda E, o=ar(Ao, Ao + TH), ek=ar(eko, eko + TH),
                    v=ar(vo, vo + TH):
                 E.tensor_tensor(o, ek, v, ALU.mult))
        p.svop("vector",
               lambda E, o=ar(Po, Po + TH), l=lamap[g], a=ar(Ao, Ao + TH):
                   E.tensor_tensor_scan(o, l, a, 0.0, ALU.mult, ALU.add))
        p.svop("vector",
               lambda E, o=ar(Qo, Qo + TH), l=lamap[g], a=ar(eko, eko + TH):
                   E.tensor_tensor_scan(o, l, a, 0.0, ALU.mult, ALU.add))
        p.op("vector",
             lambda E, o=ar(NUMo, NUMo + TO), a=ar(Ao + HALO, Ao + TH),
                    u=eu[:, g:g + 1], ps=ar(Po + HALO - 1, Po + TH - 1):
                 E.scalar_tensor_tensor(o, a, u, ps, ALU.mult, ALU.add))
        p.op("vector",
             lambda E, o=ar(DENo, DENo + TO), a=ar(eko + HALO, eko + TH),
                    u=eu[:, g:g + 1], qs=ar(Qo + HALO - 1, Qo + TH - 1):
                 E.scalar_tensor_tensor(o, a, u, qs, ALU.mult, ALU.add),
             inc=[("dve", 1)])
        d_den = p.cur("dve")
        p.op("scalar",
             lambda E, d=ar(DENo, DENo + TO): E.activation(d, d, AF.Ln),
             wait=[("dve", d_den)])
        p.op("scalar",
             lambda E, d=ar(DENo, DENo + TO):
                 E.activation(d, d, AF.Exp, scale=-1.0),
             inc=[("act", 1)])
        a_inv = p.cur("act")
        p.op("vector",
             lambda E, o=ar(vo, vo + TO), n=ar(NUMo, NUMo + TO),
                    i=ar(DENo, DENo + TO):
                 E.tensor_tensor(o, n, i, ALU.mult),
             wait=[("act", a_inv)])
        p.op("vector",
             lambda E, o=y_ap(g, 0, TO), s=ar(sro, sro + TO),
                    w=ar(vo, vo + TO):
                 E.tensor_tensor(o, s, w, ALU.mult),
             inc=[("dve", 1)])
        wkv_done = p.cur("dve")
        wkv_dones.append(wkv_done)

    if DBG:
        p.wait("sync", "dve", wkv_done)
        p.op("sync", lambda E: E.dma_start(dbg_ar[:, :], AR[:, 0:9472]),
             inc=[("sout", 16)])
        p.op("sync", lambda E: E.dma_start(dbg_y[:, :], AB[:, 0:16384]),
             inc=[("sout", 16)])
        p.op("sync", lambda E: E.dma_start(dbg_xn[:, :], xnfm[:, :]),
             inc=[("sout", 16)])
        p.op("sync", lambda E, s=meanc: E.dma_start(dbg_mr[:, 0:NTG], s[:, :]),
             inc=[("sout", 16)])
        p.op("sync", lambda E, s=rstdc: E.dma_start(dbg_mr[:, NTG:2 * NTG], s[:, :]),
             inc=[("sout", 16)])
        p.op("sync", lambda E, s=w12: E.dma_start(dbg_rt[:, :], s[:, :]),
             inc=[("sout", 16)])
        p.wait("sync", "sout", p.cur("sout"))
    def dump_acc(last_dve):
        p.wait("sync", "dve", last_dve)
        p.op("sync", lambda E: E.dma_start(dbg_acc[:, :], AR[:, 0:16384]),
             inc=[("sout", 16)])
        p.op("sync", lambda E, s=w12: E.dma_start(dbg_rt[:, :], s[:, :]),
             inc=[("sout", 16)])
        p.op("sync", lambda E, s=skipc: E.dma_start(dbg_sk[:, :], s[:, :]),
             inc=[("sout", 16)])
        for bi, buf in enumerate((lgb, eab, prb, w12)):
            p.op("sync",
                 lambda E, d=dbg_rb, bi=bi, s=buf:
                     E.dma_start(d[:, 96 * bi:96 * bi + 96], s[:, :]),
                 inc=[("sout", 16)])
        p.op("sync",
             lambda E, s=AR[0:12, LOG:LOG + 1024]: E.dma_start(dbg_lg[:, :], s),
             inc=[("sout", 16)])
        p.op("sync", lambda E, s=AB[:, 0:8192]: E.dma_start(dbg_act[:, :], s),
             inc=[("sout", 16)])
        p.op("sync",
             lambda E, s=AR[:, 17408:20480]: E.dma_start(dbg_w8b[:, :], s),
             inc=[("sout", 16)])
        p.op("sync", lambda E, s=prb: E.dma_start(dbg_pr[:, :], s[:, :]),
             inc=[("sout", 16)])
        p.wait("sync", "sout", p.cur("sout"))

    if cut("wout"):
        return finish(nc, p, ctx, emit_weight_stream)
    acc_init_done = [0] * NOG
    acc_last = [0] * NOG
    for half in range(2):
        kcs = list(range(8 * half, 8 * half + 8))
        jbase = 48
        for tg in range(NOG):
            par = tg % 2
            banks = [4 * par + h for h in range(4)]
            waits = pe_use_banks(banks)
            if tg == 0:
                waits += [slab_ready_wait(jbase + kc) for kc in kcs]
                waits += [("dve", wkv_dones[kcs[-1]])]
            first = True
            for ki, kc in enumerate(kcs):
                wslab = WS[(jbase + kc) % RING]
                for hc in range(4):
                    last = (ki == 7 and hc == 3)
                    if last and tg == NOG - 1:
                        inc = [("wfree", 8)]
                    elif last:
                        inc = [("pe", 1)]
                    else:
                        inc = []
                    p.op("tensor",
                         lambda E, ps=PS[banks[hc]],
                                w=y_ap(kc, 128 * tg, 128 * tg + 128),
                                x=wslab[:, 512 * hc:512 * hc + 512],
                                st=(ki == 0), sp=(ki == 7):
                             E.matmul(ps[:, :], w, x, start=st, stop=sp),
                         wait=waits if first else (), inc=inc)
                    first = False
            pe_tg = p.cur("pe")
            ev_wait = ("pe", pe_tg) if tg != NOG - 1 else ("wfree", p.cur("wfree"))
            for hc in range(4):
                fn_copy = (half == 0)
                inc = [("dve", 1)] if hc == 3 else []
                if fn_copy:
                    p.op("vector",
                         lambda E, o=acc_ap(tg, 512 * hc, 512 * hc + 512),
                                i=PS[banks[hc]]:
                             E.tensor_copy(o, i[:, :]),
                         wait=[ev_wait] if hc == 0 else (), inc=inc)
                else:
                    p.op("vector",
                         lambda E, o=acc_ap(tg, 512 * hc, 512 * hc + 512),
                                i=PS[banks[hc]]:
                             E.tensor_tensor(o, o, i[:, :], ALU.add),
                         wait=[ev_wait] if hc == 0 else (), inc=inc)
            banks_freed_by(banks, "dve", p.cur("dve"))
            if half == 0:
                acc_init_done[tg] = p.cur("dve")
            else:
                acc_last[tg] = p.cur("dve")

    y_free_pe = p.cur("pe")

    if DBG and cut("ffn"):
        dump_acc(max(acc_last))
    if cut("ffn"):
        return finish(nc, p, ctx, emit_weight_stream)
    prev_down_war = 0
    prev_act_war = 0
    w8bc_free = [0, 0]
    for e in range(min(NE, FFN_NE)):
        ep = e % 2
        p.op("sync",
             lambda E, d=ar(W8B[ep][0], W8B[ep][0] + 1024),
                    s=w8dram[e:e + 1, :].broadcast_to((128, 1024)):
                 E.dma_start(d, s),
             wait=[("w8b", 16), ("dve", w8bc_free[ep])], inc=[("w8b", 16)])
        w8bc_ready = p.cur("w8b")

        if FFN_PART < 2:
            continue
        silu_done = {}
        gu_pe = {}
        for ig in range(NIG):
            Q, igp = ig // 2, ig % 2
            par = ig % 2
            banks = [4 * par, 4 * par + 1, 4 * par + 2, 4 * par + 3]
            waits = pe_use_banks(banks)
            first = True
            for kc in range(16):
                j = 64 + 24 * e + 4 * Q + (kc % 4)
                if igp == 0 and kc < 4:
                    waits2 = [slab_ready_wait(j)]
                else:
                    waits2 = []
                wslab = WS[j % RING]
                cb = (kc // 4) * 512 + igp * 256
                gt = wslab[:, cb:cb + 128]
                ut = wslab[:, cb + 128:cb + 256]
                for tc in range(2):
                    xr = xnfm_ap(kc, HALO + 512 * tc, HALO + 512 * tc + 512)
                    p.op("tensor",
                         lambda E, ps=PS[banks[tc]], w=gt, x=xr,
                                st=(kc == 0), sp=(kc == 15):
                             E.matmul(ps[:, :], w, x, start=st, stop=sp),
                         wait=(waits + waits2) if first else waits2)
                    first = False
                    waits2 = []
                for tc in range(2):
                    xr = xnfm_ap(kc, HALO + 512 * tc, HALO + 512 * tc + 512)
                    last = (kc == 15 and tc == 1)
                    inc = [("pe", 1)] if last else []
                    p.op("tensor",
                         lambda E, ps=PS[banks[2 + tc]], w=ut, x=xr,
                                st=(kc == 0), sp=(kc == 15):
                             E.matmul(ps[:, :], w, x, start=st, stop=sp),
                         inc=inc)
            gu_pe[ig] = p.cur("pe")
            if FFN_PART < 3:
                banks_freed_by(banks, "pe", gu_pe[ig])
                silu_done[ig] = p.cur("dve")
                if igp == 1:
                    p.op("vector",
                         lambda E: E.tensor_copy(tmpa[:, :], tmpa[:, :]),
                         wait=[("pe", gu_pe[ig])], inc=[("wfree", 4)])
                continue
            sil_war = silu_done.get(ig - 1, prev_act_war)
            p.op("scalar",
                 lambda E, o=ar(*SIL[0]), i=PS[banks[0]]:
                     E.activation(o, i[:, :], AF.Silu),
                 wait=[("pe", gu_pe[ig]), ("dve", sil_war)],
                 inc=[("wfree", 4)] if igp == 1 else ())
            p.op("scalar",
                 lambda E, o=ar(*SIL[1]), i=PS[banks[1]]:
                     E.activation(o, i[:, :], AF.Silu),
                 inc=[("act", 1)])
            a_sil = p.cur("act")
            banks_freed_by(banks[0:2], "act", a_sil)
            if FFN_PART < 4:
                banks_freed_by(banks[2:4], "act", a_sil)
                silu_done[ig] = p.cur("dve")
                continue
            wv = [("act", a_sil), ("wfree", prev_down_war)]
            if ig == 0:
                wv.append(("w8b", w8bc_ready))
            for tc in range(2):
                p.op("vector",
                     lambda E, o=ar(*SIL[tc]), i=PS[banks[2 + tc]]:
                         E.tensor_tensor(o, o, i[:, :], ALU.mult),
                     wait=wv if tc == 0 else ())
                p.op("vector",
                     lambda E, o=act_ap(ig, 512 * tc, 512 * tc + 512),
                            a=ar(*SIL[tc]),
                            w=ar(W8B[ep][0] + 512 * tc,
                                 W8B[ep][0] + 512 * tc + 512):
                         E.tensor_tensor(o, a, w, ALU.mult),
                     inc=[("dve", 1)] if tc == 1 else [])
            silu_done[ig] = p.cur("dve")
            banks_freed_by(banks[2:4], "dve", silu_done[ig])
        w8bc_free[ep] = silu_done[NIG - 1]
        prev_act_war = silu_done[NIG - 1]
        acts_ready = silu_done[NIG - 1]

        if FFN_PART < 5:
            continue
        for tg in range(NOG):
            par = tg % 2
            banks = [4 * par + h for h in range(4)]
            waits = pe_use_banks(banks)
            if tg == 0:
                waits += [slab_ready_wait(64 + 24 * e + 16 + kc)
                          for kc in range(8)]
            first = True
            for kc in range(8):
                j = 64 + 24 * e + 16 + kc
                wslab = WS[j % RING]
                at = act_ap(kc, 128 * tg, 128 * tg + 128)
                if tg == 0:
                    waits = waits + [("dve", silu_done[kc])] if first else \
                        [("dve", silu_done[kc])]
                for hc in range(4):
                    last = (kc == 7 and hc == 3)
                    if last and tg == NOG - 1:
                        inc = [("wfree", 8)]
                    elif last:
                        inc = [("pe", 1)]
                    else:
                        inc = []
                    p.op("tensor",
                         lambda E, ps=PS[banks[hc]], w=at,
                                x=wslab[:, 512 * hc:512 * hc + 512],
                                st=(kc == 0), sp=(kc == 7):
                             E.matmul(ps[:, :], w, x, start=st, stop=sp),
                         wait=waits if (first or (tg == 0 and hc == 0)) else (),
                         inc=inc)
                    if tg == 0 and hc == 3:
                        waits = []
                    first = False
            pe_tg = p.cur("pe")
            ev_wait = ("pe", pe_tg) if tg != NOG - 1 else ("wfree", p.cur("wfree"))
            if tg == NOG - 1:
                prev_down_war = p.cur("wfree")
            for hc in range(4):
                inc = [("dve", 1)] if hc == 3 else []
                p.op("vector",
                     lambda E, o=acc_ap(tg, 512 * hc, 512 * hc + 512),
                            i=PS[banks[hc]]:
                         E.tensor_tensor(o, o, i[:, :], ALU.add),
                     wait=[ev_wait] if hc == 0 else (),
                     inc=inc)
            banks_freed_by(banks, "dve", p.cur("dve"))
            acc_last[tg] = p.cur("dve")

    if DBG and cut("final"):
        dump_acc(max(acc_last))
    if cut("final"):
        return finish(nc, p, ctx, emit_weight_stream)
    emit_weight_stream()
    xslots = [ar(*XTF[0]), ar(*XTF[1]),
              AB[:, 8192:12288].bitcast(F32), AB[:, 12288:16384].bitcast(F32)]
    slot_of = [2, 3, 0, 1]
    slot_free = {}
    slot_free[0] = ("dve", prev_act_war)
    slot_free[1] = ("dve", prev_act_war)
    slot_free[2] = ("pe", y_free_pe)
    slot_free[3] = ("pe", y_free_pe)
    d_outs = []
    a_fins = []
    for tg in range(NOG):
        sl = slot_of[tg % 4]
        xs = xslots[sl]
        p.op("sync",
             lambda E, d=xs,
                    s=x_in[HALO + 128 * tg:HALO + 128 * tg + 128, :]:
                 E.dma_start(d, s),
             wait=[slot_free[sl]], inc=[("sx", 16)])
        sx_v = p.cur("sx")
        p.op("scalar",
             lambda E, x=xs, a=afin[:, tg:tg + 1], b=bfin[:, tg:tg + 1]:
                 E.activation(x, x, AF.Identity, bias=b, scale=a),
             wait=[("sx", sx_v), ("dve", d_rt),
                   slot_free[sl] if tg < 4 else ("dve", 0)],
             inc=[("act", 1)])
        a_fin = p.cur("act")
        p.op("vector",
             lambda E, o=acc_ap(tg, 0, 2048), x=xs:
                 E.tensor_tensor(o, o, x, ALU.add),
             wait=[("act", a_fin), ("dve", acc_last[tg])], inc=[("dve", 1)])
        d_out = p.cur("dve")
        slot_free[sl] = ("dve", d_out)
        d_outs.append(d_out)
    for tg in range(NOG):
        p.op("gpsimd",
             lambda E, d=out_d[128 * tg:128 * tg + 128, :],
                    s=acc_ap(tg, 0, 2048):
                 E.dma_start(d, s),
             wait=[("dve", d_outs[tg])], inc=[("sout", 16)])
    p.wait("gpsimd", "sout", p.cur("sout"))
    p.wait("scalar", "sout", p.cur("sout"))
    p.wait("sync", "sout", p.cur("sout"))
    return finish(nc, p, ctx, emit_weight_stream)


def finish(nc, p, ctx, emit_weight_stream):
    emit_weight_stream()
    with nc.Block() as block:
        @block.tensor
        def _(e):
            for f in p.items["tensor"]:
                f(e)

        @block.vector
        def _(e):
            for f in p.items["vector"]:
                f(e)

        @block.scalar
        def _(e):
            for f in p.items["scalar"]:
                f(e)

        @block.gpsimd
        def _(e):
            for f in p.items["gpsimd"]:
                f(e)

        @block.sync
        def _(e):
            for f in p.items["sync"]:
                f(e)

    ctx.close()
    return nc


_NC_CACHE = None


def _get_nc():
    global _NC_CACHE
    if _NC_CACHE is None:
        _NC_CACHE = build_program()
    return _NC_CACHE


def kernel(L0_attn_x, L0_attn_W_rvk, L0_attn_W_out, L0_attn_time_decay,
           L0_attn_time_first, L0_ffn_classifier_w,
           L0_ffn_e_score_correction_bias, L0_ffn_gate_w, L0_ffn_up_w,
           L0_ffn_down_w):
    x = np.ascontiguousarray(np.asarray(L0_attn_x, dtype=np.float32)
                             .reshape(8192, 2048))
    W_rvk = np.asarray(L0_attn_W_rvk, dtype=np.float32)
    W_out = np.asarray(L0_attn_W_out, dtype=np.float32)
    tdv = np.asarray(L0_attn_time_decay, dtype=np.float32)
    tfv = np.asarray(L0_attn_time_first, dtype=np.float32)
    clsw = np.asarray(L0_ffn_classifier_w, dtype=np.float32)
    biasv = np.asarray(L0_ffn_e_score_correction_bias, dtype=np.float32)
    gate = np.asarray(L0_ffn_gate_w, dtype=np.float32)
    up = np.asarray(L0_ffn_up_w, dtype=np.float32)
    down = np.asarray(L0_ffn_down_w, dtype=np.float32)

    wrvk = np.ascontiguousarray(
        W_rvk.reshape(3, 16, 128, 16, 128).transpose(1, 0, 4, 3, 2)
        .reshape(16, 3, 128, 2048))
    wout = np.ascontiguousarray(W_out.T.reshape(16, 128, 2048))
    wcls = np.ascontiguousarray(
        clsw.T.reshape(16, 128, 12).transpose(1, 0, 2).reshape(128, 192))
    gt7 = gate.transpose(0, 2, 1).reshape(8, 4, 4, 128, 4, 2, 128)
    ut7 = up.transpose(0, 2, 1).reshape(8, 4, 4, 128, 4, 2, 128)
    Gt = gt7.transpose(0, 4, 2, 3, 1, 5, 6)
    Ut = ut7.transpose(0, 4, 2, 3, 1, 5, 6)
    wgu = np.ascontiguousarray(
        np.stack([Gt, Ut], axis=6).reshape(8, 16, 128, 2048))
    wdn = np.ascontiguousarray(down.transpose(0, 2, 1).reshape(8, 8, 128, 2048))
    tdm = np.ascontiguousarray(tdv.reshape(16, 128).T)
    tfm = np.ascontiguousarray(tfv.reshape(16, 128).T)
    brep = np.ascontiguousarray(np.broadcast_to(biasv[None, :], (128, 12)))
    ident = np.eye(128, dtype=np.float32)

    nc = _get_nc()
    in_maps = []
    for k in range(8):
        t0 = 1024 * k
        if k == 0:
            xc = np.concatenate(
                [np.zeros((HALO, C), np.float32), x[0:TO + 0]], axis=0)
            mk = np.zeros((128, HALO), np.float32)
        else:
            xc = x[t0 - HALO:t0 + TO]
            mk = np.ones((128, HALO), np.float32)
        in_maps.append({
            "x": np.ascontiguousarray(xc), "wrvk": wrvk, "wout": wout,
            "wcls": wcls, "wgu": wgu, "wdn": wdn, "td": tdm, "tf": tfm,
            "brep": brep, "ident": ident, "mask": mk,
        })
    res = run_bass_kernel_spmd(nc, in_maps, core_ids=list(range(8)))
    out = np.concatenate([np.asarray(r["out"]) for r in res.results], axis=0)
    return out.reshape(1, 8192, 2048)


# version 15
# speedup vs baseline: 1.1937x; 1.1937x over previous
import numpy as np
from contextlib import ExitStack

import concourse.bass as bass
import concourse.mybir as mybir
from concourse.bass_utils import run_bass_kernel_spmd

F32 = mybir.dt.float32
BF16 = mybir.dt.bfloat16
AF = mybir.ActivationFunctionType
ALU = mybir.AluOpType
AX = mybir.AxisListType

HALO, TO, TH, C = 256, 1024, 1280, 2048
NTG = 10
NOG = 8
G = 16
NE, NI, NIG = 8, 1024, 8
RING = 12
FFN_NE = 8
FFN_PART = 5
DBG = False
FINAL_POOL = False
EPS = 1e-5
NEG = -3.0e38

ENGS = ("tensor", "vector", "scalar", "gpsimd", "sync")


class Plan:
    def __init__(self):
        self.items = {e: [] for e in ENGS}
        self.cnt = {}
        self.S = {}

    def cur(self, s):
        return self.cnt.get(s, 0)

    def bump(self, s, a):
        self.cnt[s] = self.cnt.get(s, 0) + a
        return self.cnt[s]

    def op(self, eng, fn, wait=(), inc=()):
        for s, a in inc:
            self.bump(s, a)
        wait = tuple((s, v) for (s, v) in wait if v > 0)
        inc = tuple(inc)

        def run(E, fn=fn, wait=wait, inc=inc, S=self.S):
            for s, v in wait:
                E.wait_ge(S[s], v)
            ins = fn(E)
            for s, a in inc:
                ins.then_inc(S[s], a)
        self.items[eng].append(run)

    def wait(self, eng, s, v):
        if v > 0:
            self.items[eng].append(lambda E, s=s, v=v, S=self.S: E.wait_ge(S[s], v))

    def svop(self, eng, fn, wait=()):
        sem = "sv_" + eng
        self.op(eng, fn, wait=wait, inc=[(sem, 1)])
        self.wait(eng, sem, self.cur(sem))


PHASES = ("setup", "ln", "tp", "cls", "route", "attn", "wout", "ffn", "final")


def build_program(upto="final"):
    lim = PHASES.index(upto)

    def cut(name):
        return PHASES.index(name) > lim
    nc = bass.Bass("TRN2", target_bir_lowering=False)
    p = Plan()

    x_in = nc.dram_tensor("x", [TH, C], F32, kind="ExternalInput")
    wrvk = nc.dram_tensor("wrvk", [G, 3, 128, 2048], F32, kind="ExternalInput")
    wout_d = nc.dram_tensor("wout", [16, 128, 2048], F32, kind="ExternalInput")
    wcls_d = nc.dram_tensor("wcls", [128, 192], F32, kind="ExternalInput")
    wgu = nc.dram_tensor("wgu", [NE, 16, 128, 2048], F32, kind="ExternalInput")
    wdn = nc.dram_tensor("wdn", [NE, 8, 128, 2048], F32, kind="ExternalInput")
    td_d = nc.dram_tensor("td", [128, 16], F32, kind="ExternalInput")
    tf_d = nc.dram_tensor("tf", [128, 16], F32, kind="ExternalInput")
    brep_d = nc.dram_tensor("brep", [128, 12], F32, kind="ExternalInput")
    ident_d = nc.dram_tensor("ident", [128, 128], F32, kind="ExternalInput")
    mask_d = nc.dram_tensor("mask", [128, HALO], F32, kind="ExternalInput")
    out_d = nc.dram_tensor("out", [TO, C], F32, kind="ExternalOutput")
    if DBG:
        dbg_ar = nc.dram_tensor("dbg_ar", [128, 9472], F32, kind="ExternalOutput")
        dbg_y = nc.dram_tensor("dbg_y", [128, 16384], BF16, kind="ExternalOutput")
        dbg_xn = nc.dram_tensor("dbg_xn", [128, G * TH], BF16, kind="ExternalOutput")
        dbg_mr = nc.dram_tensor("dbg_mr", [128, 2 * NTG], F32, kind="ExternalOutput")
        dbg_rt = nc.dram_tensor("dbg_rt", [128, 96], F32, kind="ExternalOutput")
        dbg_acc = nc.dram_tensor("dbg_acc", [128, 16384], F32,
                                 kind="ExternalOutput")
        dbg_sk = nc.dram_tensor("dbg_sk", [128, NOG], F32, kind="ExternalOutput")
        dbg_rb = nc.dram_tensor("dbg_rb", [128, 384], F32, kind="ExternalOutput")
        dbg_lg = nc.dram_tensor("dbg_lg", [12, 1024], F32, kind="ExternalOutput")
        dbg_pr = nc.dram_tensor("dbg_pr", [128, 96], F32, kind="ExternalOutput")
        dbg_act = nc.dram_tensor("dbg_act", [128, 8192], BF16, kind="ExternalOutput")
        dbg_w8b = nc.dram_tensor("dbg_w8b", [128, 3072], F32, kind="ExternalOutput")
    schi = nc.dram_tensor("schi", [TH, C], BF16)
    w8dram = nc.dram_tensor("w8dram", [12, 1024], F32)
    sclo = nc.dram_tensor("sclo", [TO, C], BF16)

    ctx = ExitStack()
    E = ctx.enter_context

    ident = E(nc.sbuf_tensor("identsb", [128, 128], F32))
    td = E(nc.sbuf_tensor("tdsb", [128, 16], F32))
    tf = E(nc.sbuf_tensor("tfsb", [128, 16], F32))
    lam = E(nc.sbuf_tensor("lamsb", [128, 16], F32))
    eu = E(nc.sbuf_tensor("eusb", [128, 16], F32))
    brep = E(nc.sbuf_tensor("brepsb", [128, 12], F32))
    mask = E(nc.sbuf_tensor("masksb", [128, HALO], F32))
    wcls = E(nc.sbuf_tensor("wclssb", [128, 192], F32))
    clsh = E(nc.sbuf_tensor("clsh", [128, 192], BF16))
    clsl = E(nc.sbuf_tensor("clsl", [128, 192], BF16))
    ones = E(nc.sbuf_tensor("onesrow", [1, 128], F32))
    sumc = E(nc.sbuf_tensor("sumc", [128, NTG], F32))
    sqc = E(nc.sbuf_tensor("sqc", [128, NTG], F32))
    meanc = E(nc.sbuf_tensor("meanc", [128, NTG], F32))
    rstdc = E(nc.sbuf_tensor("rstdc", [128, NTG], F32))
    skipc = E(nc.sbuf_tensor("skipc", [128, NOG], F32))
    tmpa = E(nc.sbuf_tensor("tmpa", [128, 1], F32))
    tmpb = E(nc.sbuf_tensor("tmpb", [128, 1], F32))
    tmpc = E(nc.sbuf_tensor("tmpc", [128, 1], F32))
    epsc = E(nc.sbuf_tensor("epsc", [128, 1], F32))
    lgb = E(nc.sbuf_tensor("lgb", [128, 96], F32))
    eab = E(nc.sbuf_tensor("eab", [128, 96], F32))
    prb = E(nc.sbuf_tensor("prb", [128, 96], F32))
    w12 = E(nc.sbuf_tensor("w12", [128, 96], F32))
    mxc = E(nc.sbuf_tensor("mxc", [128, 8], F32))
    inv8 = E(nc.sbuf_tensor("inv8", [128, 8], F32))
    afin = E(nc.sbuf_tensor("afin", [128, 8], F32))
    bfin = E(nc.sbuf_tensor("bfin", [128, 8], F32))
    xnfm = E(nc.sbuf_tensor("xnfm", [128, G * TH], BF16))
    WS = [E(nc.sbuf_tensor(f"ws{i}", [128, 2048], BF16)) for i in range(RING)]
    AR = E(nc.sbuf_tensor("ar32", [128, 20480], F32))
    AB = E(nc.sbuf_tensor("arb16", [128, 16384], BF16))
    PS = [E(nc.psum_tensor(f"ps{i}", [128, 512], F32)) for i in range(8)]

    def ar(a, b):
        return AR[:, a:b]
    ACCo = 0
    XT = [(0, 2048), (2048, 4096)]
    XNF = [(6144, 8192), (8192, 10240)]
    LOG = 16384
    W8F = 19456
    EKo, Vo, Ao, Po, Qo = 0, 1280, 2560, 3840, 5120
    SRo, NUMo, DENo = 6400, 7424, 8448
    EK2o, V2o, SR2o = 9472, 10752, 12032
    SIL = [(16384, 16896), (16896, 17408)]
    W8B = [(17408, 18432), (18432, 19456)]
    XTF = [(16384, 18432), (18432, 20480)]

    def acc_ap(tg, a, b):
        return AR[:, 2048 * tg + a:2048 * tg + b]

    def xnh_ap(s):
        return AB[:, 2048 * s:2048 * s + 2048]

    def xnl_ap(s):
        return AB[:, 4096 + 2048 * s:4096 + 2048 * s + 2048]

    def xlr_ap(s, a, b):
        return AB[:, 1024 * s + a:1024 * s + b]

    def y_ap(kc, a, b):
        return AB[:, 1024 * kc + a:1024 * kc + b]

    def act_ap(ig, a, b):
        return AB[:, 1024 * ig + a:1024 * ig + b]

    def xnfm_ap(kc, a, b):
        return xnfm[:, TH * kc + a:TH * kc + b]

    sem_names = (["ld", "sx", "ssc", "stp", "sxl", "sout", "wfree",
                  "pe", "act", "dve", "pool", "w8b", "sv_vector", "sv_scalar"]
                 + [f"ws{i}" for i in range(RING)])
    for s in sem_names:
        p.S[s] = E(nc.semaphore(s))

    bank_free = [None] * 8

    def pe_use_banks(banks):
        w = []
        for b in banks:
            if bank_free[b] is not None:
                w.append(bank_free[b])
                bank_free[b] = None
        best = {}
        for s, v in w:
            best[s] = max(best.get(s, 0), v)
        return [(s, v) for s, v in best.items()]

    def banks_freed_by(banks, sem, val):
        for b in banks:
            bank_free[b] = (sem, val)

    slab_srcs = []
    for g in range(G):
        for s in range(3):
            slab_srcs.append(wrvk[g, s])
    for kc in range(16):
        slab_srcs.append(wout_d[kc])
    for e in range(NE):
        for kc in range(16):
            slab_srcs.append(wgu[e, kc])
        for kc in range(8):
            slab_srcs.append(wdn[e, kc])
    assert len(slab_srcs) == 256

    def emit_weight_head():
        p.wait("gpsimd", "ld", 96)
        for j in range(min(len(slab_srcs), RING)):
            p.op("gpsimd",
                 lambda E, d=WS[j % RING], s=slab_srcs[j]:
                     E.dma_start(d[:, :], s[:, :]),
                 inc=[(f"ws{j % RING}", 16)])

    ws_done = [False]

    def emit_weight_stream():
        if ws_done[0]:
            return
        ws_done[0] = True
        nfree = p.cur("wfree")
        nslab = min(len(slab_srcs), nfree + RING)
        for j in range(RING, nslab):
            p.wait("gpsimd", "wfree", j - RING + 1)
            p.op("gpsimd",
                 lambda E, d=WS[j % RING], s=slab_srcs[j]:
                     E.dma_start(d[:, :], s[:, :]),
                 inc=[(f"ws{j % RING}", 16)])

    def slab_ready_wait(j):
        return (f"ws{j % RING}", 16 * (j // RING + 1))

    for dst, src in ((ident, ident_d), (td, td_d), (tf, tf_d),
                     (brep, brep_d), (mask, mask_d), (wcls, wcls_d)):
        p.op("sync", lambda E, d=dst, s=src: E.dma_start(d[:, :], s[:, :]),
             inc=[("ld", 16)])

    p.svop("scalar", lambda E: E.activation(lam[:, :], td[:, :], AF.Exp),
           wait=[("ld", 96)])
    p.op("scalar", lambda E: E.activation(lam[:, :], lam[:, :], AF.Exp, scale=-1.0))
    p.op("scalar", lambda E: E.activation(eu[:, :], tf[:, :], AF.Exp))
    p.op("scalar", lambda E: E.activation(clsh[:, :], wcls[:, :], AF.Copy),
         inc=[("act", 1)])
    act_setup = p.cur("act")
    p.op("vector", lambda E: E.memset(ones[:, :], 1.0))
    p.op("vector", lambda E: E.memset(epsc[:, :], EPS))
    p.op("vector",
         lambda E: E.tensor_tensor(clsl[:, :], wcls[:, :], clsh[:, :], ALU.subtract),
         wait=[("act", act_setup)], inc=[("dve", 1)])
    dve_setup = p.cur("dve")

    emit_weight_head()
    if cut("ln"):
        return finish(nc, p, ctx, emit_weight_stream)
    xt_free = [0, 0]
    xnh_wr = [0, 0]
    xnl_wr = [0, 0]
    a3 = [0, 0]
    d3 = [0, 0]

    for tg in range(NTG):
        sl = tg % 2
        p.op("sync",
             lambda E, d=ar(*XT[sl]), s=x_in[128 * tg:128 * tg + 128, :]:
                 E.dma_start(d, s),
             wait=[("dve", xt_free[sl])], inc=[("sx", 16)])
        p.op("scalar",
             lambda E, x=ar(*XT[sl]), o=ar(*XNF[sl]), a=sumc[:, tg:tg + 1]:
                 E.activation(o, x, AF.Copy, accum_out=a),
             wait=[("sx", p.cur("sx")), ("dve", d3[sl])])
        p.op("scalar",
             lambda E, x=ar(*XT[sl]), o=ar(*XNF[sl]), a=sqc[:, tg:tg + 1]:
                 E.activation(o, x, AF.Square, accum_out=a),
             inc=[("act", 1)])
        a1 = p.cur("act")
        p.svop("vector",
               lambda E, o=meanc[:, tg:tg + 1], i=sumc[:, tg:tg + 1]:
                   E.tensor_scalar(o, i, 1.0 / C, None, ALU.mult),
               wait=[("act", a1)])
        p.svop("vector",
               lambda E, i=sqc[:, tg:tg + 1]:
                   E.tensor_scalar(tmpa[:, :], i, 1.0 / C, None, ALU.mult))
        p.svop("vector",
               lambda E, m=meanc[:, tg:tg + 1]:
                   E.tensor_tensor(tmpb[:, :], m, m, ALU.mult))
        p.op("vector",
             lambda E: E.tensor_tensor(tmpc[:, :], tmpa[:, :], tmpb[:, :],
                                       ALU.subtract),
             inc=[("dve", 1)])
        d1 = p.cur("dve")
        p.svop("scalar",
               lambda E: E.activation(tmpc[:, :], tmpc[:, :], AF.Ln,
                                      bias=epsc[:, 0:1]),
               wait=[("dve", d1)])
        p.op("scalar",
             lambda E, o=rstdc[:, tg:tg + 1]:
                 E.activation(o, tmpc[:, :], AF.Exp, scale=-0.5),
             inc=[("act", 1)])
        a2 = p.cur("act")
        p.op("vector",
             lambda E, o=ar(*XNF[sl]), x=ar(*XT[sl]), m=meanc[:, tg:tg + 1],
                    r=rstdc[:, tg:tg + 1]:
                 E.tensor_scalar(o, x, m, r, ALU.subtract, ALU.mult),
             wait=[("act", a2)],
             inc=[("dve", 1)])
        d2 = p.cur("dve")
        xt_free[sl] = d2
        p.op("vector",
             lambda E, o=xnh_ap(sl), i=ar(*XNF[sl]): E.tensor_copy(o, i),
             wait=[("ssc", xnh_wr[sl])], inc=[("pool", 1)])
        a3[sl] = p.cur("pool")
        if tg >= 2:
            p.op("vector",
                 lambda E, o=xnl_ap(sl), i=ar(*XNF[sl]), h=xnh_ap(sl):
                     E.tensor_tensor(o, i, h, ALU.subtract),
                 wait=[("pool", a3[sl]), ("ssc", xnl_wr[sl])], inc=[("dve", 1)])
            d3[sl] = p.cur("dve")
        p.op("gpsimd",
             lambda E, d=schi[128 * tg:128 * tg + 128, :], s=xnh_ap(sl):
                 E.dma_start(d, s),
             wait=[("pool", a3[sl])], inc=[("ssc", 16)])
        xnh_wr[sl] = p.cur("ssc")
        if tg >= 2:
            p.op("gpsimd",
                 lambda E, d=sclo[128 * (tg - 2):128 * (tg - 2) + 128, :],
                        s=xnl_ap(sl):
                     E.dma_start(d, s),
                 wait=[("dve", d3[sl])], inc=[("ssc", 16)])
            xnl_wr[sl] = p.cur("ssc")

    if cut("tp"):
        return finish(nc, p, ctx, emit_weight_stream)
    p.wait("sync", "ssc", p.cur("ssc"))
    for g in range(G):
        p.op("sync",
             lambda E, d=xnfm_ap(g, 0, TH), s=schi[0:TH, 128 * g:128 * g + 128]:
                 E.dma_start(d, s, transpose=True),
             inc=[("stp", 16)])
    stp_all = p.cur("stp")

    xlr_free = [0, 0, 0, 0]

    if cut("cls"):
        return finish(nc, p, ctx, emit_weight_stream)
    cls_banks = [0, 1]
    w0 = pe_use_banks(cls_banks)
    for g in range(G):
        p.op("sync",
             lambda E, d=xlr_ap(g % 4, 0, 1024),
                    s=sclo[0:TO, 128 * g:128 * g + 128]:
                 E.dma_start(d, s, transpose=True),
             wait=[("pe", xlr_free[g % 4])], inc=[("sxl", 16)])
        waits = [("sxl", p.cur("sxl"))]
        if g == 0:
            waits += [("stp", stp_all), ("act", act_setup), ("dve", dve_setup)]
            waits += w0
        for tc in range(2):
            ps = PS[cls_banks[tc]]
            hi = clsh[:, 12 * g:12 * g + 12]
            lo = clsl[:, 12 * g:12 * g + 12]
            xh = xnfm_ap(g, HALO + 512 * tc, HALO + 512 * tc + 512)
            xl = xlr_ap(g % 4, 512 * tc, 512 * tc + 512)
            p.op("tensor",
                 lambda E, ps=ps, w=hi, x=xh, st=(g == 0):
                     E.matmul(ps[0:12, :], w, x, start=st, stop=False),
                 wait=waits if tc == 0 else ())
            p.op("tensor",
                 lambda E, ps=ps, w=lo, x=xh:
                     E.matmul(ps[0:12, :], w, x, start=False, stop=False))
            inc = [("pe", 1)] if tc == 1 else []
            p.op("tensor",
                 lambda E, ps=ps, w=hi, x=xl, sp=(g == G - 1):
                     E.matmul(ps[0:12, :], w, x, start=False, stop=sp),
                 inc=inc)
        xlr_free[g % 4] = p.cur("pe")
    cls_done = p.cur("pe")
    p.op("scalar",
         lambda E, o=AR[0:12, LOG:LOG + 512]:
             E.activation(o, PS[0][0:12, :], AF.Copy),
         wait=[("pe", cls_done)])
    p.op("scalar",
         lambda E, o=AR[0:12, LOG + 512:LOG + 1024]:
             E.activation(o, PS[1][0:12, :], AF.Copy),
         inc=[("act", 1)])
    lg_evac = p.cur("act")
    banks_freed_by([0, 1], "act", lg_evac)

    w1 = pe_use_banks([2])
    for tg in range(NOG):
        inc = [("pe", 1)] if tg == NOG - 1 else []
        p.op("tensor",
             lambda E, o=PS[2][:, 12 * tg:12 * tg + 12],
                    i=AR[0:12, LOG + 128 * tg:LOG + 128 * tg + 128]:
                 E.transpose(o, i, ident[0:12, 0:12]),
             wait=(w1 + [("act", lg_evac)]) if tg == 0 else (), inc=inc)
    lgT_done = p.cur("pe")
    p.op("vector",
         lambda E: E.tensor_copy(lgb[:, :], PS[2][:, 0:96]),
         wait=[("pe", lgT_done)], inc=[("dve", 1)])
    banks_freed_by([2], "dve", p.cur("dve"))

    def b3(buf):
        return buf[:, :].rearrange("p (b t) -> p b t", t=12)

    def bc8(col8):
        return col8[:, :].unsqueeze(2).broadcast_to((128, 8, 12))

    p.svop("vector", lambda E: E.reduce_max(mxc[:, :], b3(lgb), axis=AX.X))
    p.op("vector",
         lambda E: E.tensor_scalar(mxc[:, :], mxc[:, :], -1.0, None, ALU.mult),
         inc=[("dve", 1)])
    d_mx = p.cur("dve")
    for tg in range(NOG):
        s = slice(12 * tg, 12 * tg + 12)
        inc = [("act", 1)] if tg == NOG - 1 else []
        p.op("scalar",
             lambda E, o=eab[:, s], i=lgb[:, s], b=mxc[:, tg:tg + 1]:
                 E.activation(o, i, AF.Exp, bias=b),
             wait=[("dve", d_mx)] if tg == 0 else (), inc=inc)
    a_exp = p.cur("act")
    p.svop("vector", lambda E: E.reduce_sum(mxc[:, :], b3(eab), axis=AX.X),
           wait=[("act", a_exp)])
    p.svop("vector", lambda E: E.reciprocal(inv8[:, :], mxc[:, :]))
    p.op("vector",
         lambda E: E.tensor_tensor(b3(prb), b3(eab), bc8(inv8), ALU.mult))
    p.op("vector",
         lambda E, bb=brep[:, :].unsqueeze(1).broadcast_to((128, 8, 12)):
             E.tensor_tensor(b3(lgb), b3(prb), bb, ALU.add))
    p.svop("vector", lambda E: E.reduce_max(mxc[:, :], b3(lgb), axis=AX.X))
    p.op("vector",
         lambda E: E.tensor_tensor(b3(w12), b3(lgb), bc8(mxc), ALU.is_ge))
    p.op("vector",
         lambda E: E.scalar_tensor_tensor(b3(w12), b3(w12), NEG, b3(lgb),
                                          ALU.mult, ALU.add))
    p.svop("vector", lambda E: E.reduce_max(inv8[:, :], b3(w12), axis=AX.X))
    p.op("vector",
         lambda E: E.tensor_tensor(b3(w12), b3(lgb), bc8(inv8), ALU.is_ge))
    p.op("vector",
         lambda E: E.scalar_tensor_tensor(b3(w12), b3(prb), 1.5, b3(w12),
                                          ALU.mult, ALU.mult))
    p.svop("vector",
           lambda E, i=b3(w12)[:, :, 8:12]:
               E.reduce_sum(skipc[:, :], i, axis=AX.X))
    p.svop("vector",
           lambda E: E.tensor_tensor(afin[:, :], rstdc[:, 2:10], skipc[:, :],
                                     ALU.mult))
    p.svop("vector",
           lambda E: E.tensor_tensor(bfin[:, :], meanc[:, 2:10], afin[:, :],
                                     ALU.mult))
    p.svop("vector",
           lambda E: E.tensor_scalar(bfin[:, :], bfin[:, :], -1.0, None,
                                     ALU.mult))
    p.svop("vector",
           lambda E: E.tensor_scalar(afin[:, :], afin[:, :], 1.0, None,
                                     ALU.add))
    p.op("vector", lambda E: E.tensor_copy(tmpa[:, :], tmpa[:, :]),
         inc=[("dve", 1)])
    d_rt = p.cur("dve")

    if cut("attn"):
        return finish(nc, p, ctx, emit_weight_stream)
    FULL_CH = [(0, 512), (512, 1024), (1024, 1280)]
    OWN_CH = [(HALO, HALO + 512), (HALO + 512, TH)]
    lamap = {g: lam[:, g:g + 1].broadcast_to((128, TH)) for g in range(G)}
    wkv_done = 0
    wkv_dones = []
    sub_banks = {"r": [0, 1], "v": [2, 3, 4], "k": [5, 6, 7]}

    for g in range(G):
        eko = EKo if g % 2 == 0 else EK2o
        vo = Vo if g % 2 == 0 else V2o
        sro = SRo if g % 2 == 0 else SR2o
        evac_acts = {}
        for si, (sname, chunks) in enumerate(
                (("r", OWN_CH), ("v", FULL_CH), ("k", FULL_CH))):
            j = 3 * g + si
            banks = sub_banks[sname]
            waits = [slab_ready_wait(j)] + pe_use_banks(banks)
            if g == 0 and si == 0:
                waits.append(("stp", stp_all))
            wslab = WS[j % RING]
            nkc = 16
            for kc in range(nkc):
                wt = wslab[:, 128 * kc:128 * kc + 128]
                for ci, (c0, c1) in enumerate(chunks):
                    last = (kc == nkc - 1 and ci == len(chunks) - 1)
                    inc = ([("pe", 1)] if last else [])
                    p.op("tensor",
                         lambda E, ps=PS[banks[ci]], w=wt,
                                x=xnfm_ap(kc, c0, c1), st=(kc == 0),
                                sp=(kc == nkc - 1), n=c1 - c0:
                             E.matmul(ps[:, 0:n], w, x, start=st, stop=sp),
                         wait=waits if kc == 0 and ci == 0 else (), inc=inc)
            evac_acts[sname] = p.cur("pe")

        war = [("dve", wkv_dones[g - 2] if g >= 2 else 0)]
        p.op("scalar",
             lambda E, o=ar(sro, sro + 512), i=PS[0][:, :]:
                 E.activation(o, i, AF.Sigmoid),
             wait=[("pe", evac_acts["r"])] + war, inc=[("wfree", 1)])
        p.op("scalar",
             lambda E, o=ar(sro + 512, sro + 1024), i=PS[1][:, :]:
                 E.activation(o, i, AF.Sigmoid),
             inc=[("act", 1)])
        banks_freed_by([0, 1], "act", p.cur("act"))
        p.wait("scalar", "pe", evac_acts["v"])
        for ci, (c0, c1) in enumerate(FULL_CH):
            inc = [("wfree", 1)] if ci == 0 else ([("act", 1)] if ci == 2 else [])
            p.op("scalar",
                 lambda E, o=ar(vo + c0, vo + c1), i=PS[2 + ci][:, 0:c1 - c0]:
                     E.activation(o, i, AF.Copy),
                 inc=inc)
        banks_freed_by([2, 3, 4], "act", p.cur("act"))
        p.wait("scalar", "pe", evac_acts["k"])
        for ci, (c0, c1) in enumerate(FULL_CH):
            inc = [("wfree", 1)] if ci == 0 else ([("act", 1)] if ci == 2 else [])
            p.op("scalar",
                 lambda E, o=ar(eko + c0, eko + c1), i=PS[5 + ci][:, 0:c1 - c0]:
                     E.activation(o, i, AF.Exp),
                 inc=inc)
        banks_freed_by([5, 6, 7], "act", p.cur("act"))
        a_ev = p.cur("act")

        if g == G - 1:
            w2 = pe_use_banks([3, 4])
            for tg8 in range(NOG):
                b = 3 + tg8 // 4
                inc = [("pe", 1)] if tg8 == NOG - 1 else []
                p.op("tensor",
                     lambda E, o=PS[b][0:12,
                                       128 * (tg8 % 4):128 * (tg8 % 4) + 128],
                            i=w12[:, 12 * tg8:12 * tg8 + 12]:
                         E.transpose(o, i, ident[:, :]),
                     wait=(w2 + [("dve", d_rt)]) if tg8 == 0 else (), inc=inc)
            w8T_done = p.cur("pe")
            p.op("scalar",
                 lambda E, o=AR[0:12, W8F:W8F + 512]:
                     E.activation(o, PS[3][0:12, :], AF.Copy),
                 wait=[("pe", w8T_done)])
            p.op("scalar",
                 lambda E, o=AR[0:12, W8F + 512:W8F + 1024]:
                     E.activation(o, PS[4][0:12, :], AF.Copy),
                 inc=[("act", 1)])
            w8fm_ready = p.cur("act")
            banks_freed_by([3, 4], "act", w8fm_ready)
            p.op("sync",
                 lambda E, s=AR[0:12, W8F:W8F + 1024]:
                     E.dma_start(w8dram[:, :], s),
                 wait=[("act", w8fm_ready)], inc=[("w8b", 16)])

        p.op("vector",
             lambda E, ek=ar(eko, eko + HALO):
                 E.tensor_tensor(ek, ek, mask[:, :], ALU.mult),
             wait=[("act", a_ev)])
        p.op("vector",
             lambda E, o=ar(Ao, Ao + TH), ek=ar(eko, eko + TH),
                    v=ar(vo, vo + TH):
                 E.tensor_tensor(o, ek, v, ALU.mult))
        p.svop("vector",
               lambda E, o=ar(Po, Po + TH), l=lamap[g], a=ar(Ao, Ao + TH):
                   E.tensor_tensor_scan(o, l, a, 0.0, ALU.mult, ALU.add))
        p.svop("vector",
               lambda E, o=ar(Qo, Qo + TH), l=lamap[g], a=ar(eko, eko + TH):
                   E.tensor_tensor_scan(o, l, a, 0.0, ALU.mult, ALU.add))
        p.op("vector",
             lambda E, o=ar(NUMo, NUMo + TO), a=ar(Ao + HALO, Ao + TH),
                    u=eu[:, g:g + 1], ps=ar(Po + HALO - 1, Po + TH - 1):
                 E.scalar_tensor_tensor(o, a, u, ps, ALU.mult, ALU.add))
        p.op("vector",
             lambda E, o=ar(DENo, DENo + TO), a=ar(eko + HALO, eko + TH),
                    u=eu[:, g:g + 1], qs=ar(Qo + HALO - 1, Qo + TH - 1):
                 E.scalar_tensor_tensor(o, a, u, qs, ALU.mult, ALU.add),
             inc=[("dve", 1)])
        d_den = p.cur("dve")
        p.op("scalar",
             lambda E, d=ar(DENo, DENo + TO): E.activation(d, d, AF.Ln),
             wait=[("dve", d_den)])
        p.op("scalar",
             lambda E, d=ar(DENo, DENo + TO):
                 E.activation(d, d, AF.Exp, scale=-1.0),
             inc=[("act", 1)])
        a_inv = p.cur("act")
        p.op("vector",
             lambda E, o=ar(vo, vo + TO), n=ar(NUMo, NUMo + TO),
                    i=ar(DENo, DENo + TO):
                 E.tensor_tensor(o, n, i, ALU.mult),
             wait=[("act", a_inv)])
        p.op("vector",
             lambda E, o=y_ap(g, 0, TO), s=ar(sro, sro + TO),
                    w=ar(vo, vo + TO):
                 E.tensor_tensor(o, s, w, ALU.mult),
             inc=[("dve", 1)])
        wkv_done = p.cur("dve")
        wkv_dones.append(wkv_done)

    if DBG:
        p.wait("sync", "dve", wkv_done)
        p.op("sync", lambda E: E.dma_start(dbg_ar[:, :], AR[:, 0:9472]),
             inc=[("sout", 16)])
        p.op("sync", lambda E: E.dma_start(dbg_y[:, :], AB[:, 0:16384]),
             inc=[("sout", 16)])
        p.op("sync", lambda E: E.dma_start(dbg_xn[:, :], xnfm[:, :]),
             inc=[("sout", 16)])
        p.op("sync", lambda E, s=meanc: E.dma_start(dbg_mr[:, 0:NTG], s[:, :]),
             inc=[("sout", 16)])
        p.op("sync", lambda E, s=rstdc: E.dma_start(dbg_mr[:, NTG:2 * NTG], s[:, :]),
             inc=[("sout", 16)])
        p.op("sync", lambda E, s=w12: E.dma_start(dbg_rt[:, :], s[:, :]),
             inc=[("sout", 16)])
        p.wait("sync", "sout", p.cur("sout"))
    def dump_acc(last_dve):
        p.wait("sync", "dve", last_dve)
        p.op("sync", lambda E: E.dma_start(dbg_acc[:, :], AR[:, 0:16384]),
             inc=[("sout", 16)])
        p.op("sync", lambda E, s=w12: E.dma_start(dbg_rt[:, :], s[:, :]),
             inc=[("sout", 16)])
        p.op("sync", lambda E, s=skipc: E.dma_start(dbg_sk[:, :], s[:, :]),
             inc=[("sout", 16)])
        for bi, buf in enumerate((lgb, eab, prb, w12)):
            p.op("sync",
                 lambda E, d=dbg_rb, bi=bi, s=buf:
                     E.dma_start(d[:, 96 * bi:96 * bi + 96], s[:, :]),
                 inc=[("sout", 16)])
        p.op("sync",
             lambda E, s=AR[0:12, LOG:LOG + 1024]: E.dma_start(dbg_lg[:, :], s),
             inc=[("sout", 16)])
        p.op("sync", lambda E, s=AB[:, 0:8192]: E.dma_start(dbg_act[:, :], s),
             inc=[("sout", 16)])
        p.op("sync",
             lambda E, s=AR[:, 17408:20480]: E.dma_start(dbg_w8b[:, :], s),
             inc=[("sout", 16)])
        p.op("sync", lambda E, s=prb: E.dma_start(dbg_pr[:, :], s[:, :]),
             inc=[("sout", 16)])
        p.wait("sync", "sout", p.cur("sout"))

    if cut("wout"):
        return finish(nc, p, ctx, emit_weight_stream)
    acc_init_done = [0] * NOG
    acc_last = [0] * NOG
    for half in range(2):
        kcs = list(range(8 * half, 8 * half + 8))
        jbase = 48
        for tg in range(NOG):
            par = tg % 2
            banks = [4 * par + h for h in range(4)]
            waits = pe_use_banks(banks)
            if tg == 0:
                waits += [slab_ready_wait(jbase + kc) for kc in kcs]
                waits += [("dve", wkv_dones[kcs[-1]])]
            first = True
            for ki, kc in enumerate(kcs):
                wslab = WS[(jbase + kc) % RING]
                for hc in range(4):
                    last = (ki == 7 and hc == 3)
                    if last and tg == NOG - 1:
                        inc = [("wfree", 8)]
                    elif last:
                        inc = [("pe", 1)]
                    else:
                        inc = []
                    p.op("tensor",
                         lambda E, ps=PS[banks[hc]],
                                w=y_ap(kc, 128 * tg, 128 * tg + 128),
                                x=wslab[:, 512 * hc:512 * hc + 512],
                                st=(ki == 0), sp=(ki == 7):
                             E.matmul(ps[:, :], w, x, start=st, stop=sp),
                         wait=waits if first else (), inc=inc)
                    first = False
            pe_tg = p.cur("pe")
            ev_wait = ("pe", pe_tg) if tg != NOG - 1 else ("wfree", p.cur("wfree"))
            for hc in range(4):
                fn_copy = (half == 0)
                inc = [("dve", 1)] if hc == 3 else []
                if fn_copy:
                    p.op("vector",
                         lambda E, o=acc_ap(tg, 512 * hc, 512 * hc + 512),
                                i=PS[banks[hc]]:
                             E.tensor_copy(o, i[:, :]),
                         wait=[ev_wait] if hc == 0 else (), inc=inc)
                else:
                    p.op("vector",
                         lambda E, o=acc_ap(tg, 512 * hc, 512 * hc + 512),
                                i=PS[banks[hc]]:
                             E.tensor_tensor(o, o, i[:, :], ALU.add),
                         wait=[ev_wait] if hc == 0 else (), inc=inc)
            banks_freed_by(banks, "dve", p.cur("dve"))
            if half == 0:
                acc_init_done[tg] = p.cur("dve")
            else:
                acc_last[tg] = p.cur("dve")

    y_free_pe = p.cur("pe")

    if DBG and cut("ffn"):
        dump_acc(max(acc_last))
    if cut("ffn"):
        return finish(nc, p, ctx, emit_weight_stream)
    fin_state = {"xtf_free": {}, "d_outs": {}, "emitted": set()}

    def emit_final(tg, xslots, slot_of):
        sl = slot_of[tg % 4]
        xs = xslots[sl]
        p.op("sync",
             lambda E, d=xs,
                    s=x_in[HALO + 128 * tg:HALO + 128 * tg + 128, :]:
                 E.dma_start(d, s),
             wait=[fin_state["xtf_free"][sl]], inc=[("sx", 16)])
        sx_v = p.cur("sx")
        p.op("scalar",
             lambda E, x=xs, a=afin[:, tg:tg + 1], b=bfin[:, tg:tg + 1]:
                 E.activation(x, x, AF.Identity, bias=b, scale=a),
             wait=[("sx", sx_v), ("dve", d_rt)],
             inc=[("act", 1)])
        a_fin = p.cur("act")
        p.op("vector",
             lambda E, o=acc_ap(tg, 0, 2048), x=xs:
                 E.tensor_tensor(o, o, x, ALU.add),
             wait=[("act", a_fin), ("dve", acc_last[tg])], inc=[("dve", 1)])
        d_out = p.cur("dve")
        fin_state["xtf_free"][sl] = ("dve", d_out)
        fin_state["d_outs"][tg] = d_out
        fin_state["emitted"].add(tg)

    prev_down_war = 0
    prev_act_war = 0
    w8bc_free = [0, 0]
    for e in range(min(NE, FFN_NE)):
        ep = e % 2
        p.op("sync",
             lambda E, d=ar(W8B[ep][0], W8B[ep][0] + 1024),
                    s=w8dram[e:e + 1, :].broadcast_to((128, 1024)):
                 E.dma_start(d, s),
             wait=[("w8b", 16), ("dve", w8bc_free[ep])], inc=[("w8b", 16)])
        w8bc_ready = p.cur("w8b")

        if FFN_PART < 2:
            continue
        silu_done = {}
        gu_pe = {}
        for ig in range(NIG):
            Q, igp = ig // 2, ig % 2
            par = ig % 2
            banks = [4 * par, 4 * par + 1, 4 * par + 2, 4 * par + 3]
            waits = pe_use_banks(banks)
            first = True
            for kc in range(16):
                j = 64 + 24 * e + 4 * Q + (kc % 4)
                if igp == 0 and kc < 4:
                    waits2 = [slab_ready_wait(j)]
                else:
                    waits2 = []
                wslab = WS[j % RING]
                cb = (kc // 4) * 512 + igp * 256
                gt = wslab[:, cb:cb + 128]
                ut = wslab[:, cb + 128:cb + 256]
                for tc in range(2):
                    xr = xnfm_ap(kc, HALO + 512 * tc, HALO + 512 * tc + 512)
                    p.op("tensor",
                         lambda E, ps=PS[banks[tc]], w=gt, x=xr,
                                st=(kc == 0), sp=(kc == 15):
                             E.matmul(ps[:, :], w, x, start=st, stop=sp),
                         wait=(waits + waits2) if first else waits2)
                    first = False
                    waits2 = []
                for tc in range(2):
                    xr = xnfm_ap(kc, HALO + 512 * tc, HALO + 512 * tc + 512)
                    last = (kc == 15 and tc == 1)
                    inc = [("pe", 1)] if last else []
                    p.op("tensor",
                         lambda E, ps=PS[banks[2 + tc]], w=ut, x=xr,
                                st=(kc == 0), sp=(kc == 15):
                             E.matmul(ps[:, :], w, x, start=st, stop=sp),
                         inc=inc)
            gu_pe[ig] = p.cur("pe")
            if FFN_PART < 3:
                banks_freed_by(banks, "pe", gu_pe[ig])
                silu_done[ig] = p.cur("dve")
                if igp == 1:
                    p.op("vector",
                         lambda E: E.tensor_copy(tmpa[:, :], tmpa[:, :]),
                         wait=[("pe", gu_pe[ig])], inc=[("wfree", 4)])
                continue
            sil_war = silu_done.get(ig - 1, prev_act_war)
            p.op("scalar",
                 lambda E, o=ar(*SIL[0]), i=PS[banks[0]]:
                     E.activation(o, i[:, :], AF.Silu),
                 wait=[("pe", gu_pe[ig]), ("dve", sil_war)],
                 inc=[("wfree", 4)] if igp == 1 else ())
            p.op("scalar",
                 lambda E, o=ar(*SIL[1]), i=PS[banks[1]]:
                     E.activation(o, i[:, :], AF.Silu),
                 inc=[("act", 1)])
            a_sil = p.cur("act")
            banks_freed_by(banks[0:2], "act", a_sil)
            if FFN_PART < 4:
                banks_freed_by(banks[2:4], "act", a_sil)
                silu_done[ig] = p.cur("dve")
                continue
            wv = [("act", a_sil), ("wfree", prev_down_war)]
            if ig == 0:
                wv.append(("w8b", w8bc_ready))
            for tc in range(2):
                p.op("vector",
                     lambda E, o=ar(*SIL[tc]), i=PS[banks[2 + tc]]:
                         E.tensor_tensor(o, o, i[:, :], ALU.mult),
                     wait=wv if tc == 0 else ())
                p.op("vector",
                     lambda E, o=act_ap(ig, 512 * tc, 512 * tc + 512),
                            a=ar(*SIL[tc]),
                            w=ar(W8B[ep][0] + 512 * tc,
                                 W8B[ep][0] + 512 * tc + 512):
                         E.tensor_tensor(o, a, w, ALU.mult),
                     inc=[("dve", 1)] if tc == 1 else [])
            silu_done[ig] = p.cur("dve")
            banks_freed_by(banks[2:4], "dve", silu_done[ig])
        w8bc_free[ep] = silu_done[NIG - 1]
        prev_act_war = silu_done[NIG - 1]
        acts_ready = silu_done[NIG - 1]

        if e == NE - 1 and FFN_PART >= 5 and not cut("final"):
            emit_weight_stream()
            xslots_e = [None, None,
                        AB[:, 8192:12288].bitcast(F32),
                        AB[:, 12288:16384].bitcast(F32)]
            slot_of_e = [2, 3, 2, 3]
            fin_state["xtf_free"][2] = ("pe", y_free_pe)
            fin_state["xtf_free"][3] = ("pe", y_free_pe)
            for tg in range(4):
                emit_final(tg, xslots_e, slot_of_e)

        if FFN_PART < 5:
            continue
        for tg in range(NOG):
            par = tg % 2
            banks = [4 * par + h for h in range(4)]
            waits = pe_use_banks(banks)
            if tg == 0:
                waits += [slab_ready_wait(64 + 24 * e + 16 + kc)
                          for kc in range(8)]
            first = True
            for kc in range(8):
                j = 64 + 24 * e + 16 + kc
                wslab = WS[j % RING]
                at = act_ap(kc, 128 * tg, 128 * tg + 128)
                if tg == 0:
                    waits = waits + [("dve", silu_done[kc])] if first else \
                        [("dve", silu_done[kc])]
                for hc in range(4):
                    last = (kc == 7 and hc == 3)
                    if last and tg == NOG - 1:
                        inc = [("wfree", 8)]
                    elif last:
                        inc = [("pe", 1)]
                    else:
                        inc = []
                    p.op("tensor",
                         lambda E, ps=PS[banks[hc]], w=at,
                                x=wslab[:, 512 * hc:512 * hc + 512],
                                st=(kc == 0), sp=(kc == 7):
                             E.matmul(ps[:, :], w, x, start=st, stop=sp),
                         wait=waits if (first or (tg == 0 and hc == 0)) else (),
                         inc=inc)
                    if tg == 0 and hc == 3:
                        waits = []
                    first = False
            pe_tg = p.cur("pe")
            ev_wait = ("pe", pe_tg) if tg != NOG - 1 else ("wfree", p.cur("wfree"))
            if tg == NOG - 1:
                prev_down_war = p.cur("wfree")
            for hc in range(4):
                inc = [("dve", 1)] if hc == 3 else []
                p.op("vector",
                     lambda E, o=acc_ap(tg, 512 * hc, 512 * hc + 512),
                            i=PS[banks[hc]]:
                         E.tensor_tensor(o, o, i[:, :], ALU.add),
                     wait=[ev_wait] if hc == 0 else (),
                     inc=inc)
            banks_freed_by(banks, "dve", p.cur("dve"))
            acc_last[tg] = p.cur("dve")

    if DBG and cut("final"):
        dump_acc(max(acc_last))
    if cut("final"):
        return finish(nc, p, ctx, emit_weight_stream)
    emit_weight_stream()
    xslots = [ar(*XTF[0]), ar(*XTF[1]),
              AB[:, 8192:12288].bitcast(F32), AB[:, 12288:16384].bitcast(F32)]
    slot_of = [2, 3, 0, 1]
    for sl, ev in ((0, ("dve", prev_act_war)), (1, ("dve", prev_act_war))):
        fin_state["xtf_free"].setdefault(sl, ev)
    fin_state["xtf_free"].setdefault(2, ("pe", y_free_pe))
    fin_state["xtf_free"].setdefault(3, ("pe", y_free_pe))
    for tg in range(NOG):
        if tg in fin_state["emitted"]:
            continue
        emit_final(tg, xslots, slot_of)
    for tg in range(NOG):
        p.op("gpsimd",
             lambda E, d=out_d[128 * tg:128 * tg + 128, :],
                    s=acc_ap(tg, 0, 2048):
                 E.dma_start(d, s),
             wait=[("dve", fin_state["d_outs"][tg]), ("dve", acc_last[tg])],
             inc=[("sout", 16)])
    p.wait("gpsimd", "sout", p.cur("sout"))
    p.wait("scalar", "sout", p.cur("sout"))
    p.wait("sync", "sout", p.cur("sout"))
    return finish(nc, p, ctx, emit_weight_stream)


def finish(nc, p, ctx, emit_weight_stream):
    emit_weight_stream()
    with nc.Block() as block:
        @block.tensor
        def _(e):
            for f in p.items["tensor"]:
                f(e)

        @block.vector
        def _(e):
            for f in p.items["vector"]:
                f(e)

        @block.scalar
        def _(e):
            for f in p.items["scalar"]:
                f(e)

        @block.gpsimd
        def _(e):
            for f in p.items["gpsimd"]:
                f(e)

        @block.sync
        def _(e):
            for f in p.items["sync"]:
                f(e)

    ctx.close()
    return nc


_NC_CACHE = None


def _get_nc():
    global _NC_CACHE
    if _NC_CACHE is None:
        _NC_CACHE = build_program()
    return _NC_CACHE


def kernel(L0_attn_x, L0_attn_W_rvk, L0_attn_W_out, L0_attn_time_decay,
           L0_attn_time_first, L0_ffn_classifier_w,
           L0_ffn_e_score_correction_bias, L0_ffn_gate_w, L0_ffn_up_w,
           L0_ffn_down_w):
    x = np.ascontiguousarray(np.asarray(L0_attn_x, dtype=np.float32)
                             .reshape(8192, 2048))
    W_rvk = np.asarray(L0_attn_W_rvk, dtype=np.float32)
    W_out = np.asarray(L0_attn_W_out, dtype=np.float32)
    tdv = np.asarray(L0_attn_time_decay, dtype=np.float32)
    tfv = np.asarray(L0_attn_time_first, dtype=np.float32)
    clsw = np.asarray(L0_ffn_classifier_w, dtype=np.float32)
    biasv = np.asarray(L0_ffn_e_score_correction_bias, dtype=np.float32)
    gate = np.asarray(L0_ffn_gate_w, dtype=np.float32)
    up = np.asarray(L0_ffn_up_w, dtype=np.float32)
    down = np.asarray(L0_ffn_down_w, dtype=np.float32)

    wrvk = np.ascontiguousarray(
        W_rvk.reshape(3, 16, 128, 16, 128).transpose(1, 0, 4, 3, 2)
        .reshape(16, 3, 128, 2048))
    wout = np.ascontiguousarray(W_out.T.reshape(16, 128, 2048))
    wcls = np.ascontiguousarray(
        clsw.T.reshape(16, 128, 12).transpose(1, 0, 2).reshape(128, 192))
    gt7 = gate.transpose(0, 2, 1).reshape(8, 4, 4, 128, 4, 2, 128)
    ut7 = up.transpose(0, 2, 1).reshape(8, 4, 4, 128, 4, 2, 128)
    Gt = gt7.transpose(0, 4, 2, 3, 1, 5, 6)
    Ut = ut7.transpose(0, 4, 2, 3, 1, 5, 6)
    wgu = np.ascontiguousarray(
        np.stack([Gt, Ut], axis=6).reshape(8, 16, 128, 2048))
    wdn = np.ascontiguousarray(down.transpose(0, 2, 1).reshape(8, 8, 128, 2048))
    tdm = np.ascontiguousarray(tdv.reshape(16, 128).T)
    tfm = np.ascontiguousarray(tfv.reshape(16, 128).T)
    brep = np.ascontiguousarray(np.broadcast_to(biasv[None, :], (128, 12)))
    ident = np.eye(128, dtype=np.float32)

    nc = _get_nc()
    in_maps = []
    for k in range(8):
        t0 = 1024 * k
        if k == 0:
            xc = np.concatenate(
                [np.zeros((HALO, C), np.float32), x[0:TO + 0]], axis=0)
            mk = np.zeros((128, HALO), np.float32)
        else:
            xc = x[t0 - HALO:t0 + TO]
            mk = np.ones((128, HALO), np.float32)
        in_maps.append({
            "x": np.ascontiguousarray(xc), "wrvk": wrvk, "wout": wout,
            "wcls": wcls, "wgu": wgu, "wdn": wdn, "td": tdm, "tf": tfm,
            "brep": brep, "ident": ident, "mask": mk,
        })
    res = run_bass_kernel_spmd(nc, in_maps, core_ids=list(range(8)))
    out = np.concatenate([np.asarray(r["out"]) for r in res.results], axis=0)
    return out.reshape(1, 8192, 2048)


# version 17 (final)
# speedup vs baseline: 1.1995x; 1.0048x over previous
import numpy as np
from contextlib import ExitStack

import concourse.bass as bass
import concourse.mybir as mybir
from concourse.bass_utils import run_bass_kernel_spmd

F32 = mybir.dt.float32
BF16 = mybir.dt.bfloat16
AF = mybir.ActivationFunctionType
ALU = mybir.AluOpType
AX = mybir.AxisListType

HALO, TO, TH, C = 256, 1024, 1280, 2048
NTG = 10
NOG = 8
G = 16
NE, NI, NIG = 8, 1024, 8
RING = 12
FFN_NE = 8
FFN_PART = 5
DBG = False
FINAL_POOL = False
EPS = 1e-5
NEG = -3.0e38

ENGS = ("tensor", "vector", "scalar", "gpsimd", "sync")


class Plan:
    def __init__(self):
        self.items = {e: [] for e in ENGS}
        self.cnt = {}
        self.S = {}

    def cur(self, s):
        return self.cnt.get(s, 0)

    def bump(self, s, a):
        self.cnt[s] = self.cnt.get(s, 0) + a
        return self.cnt[s]

    def op(self, eng, fn, wait=(), inc=()):
        for s, a in inc:
            self.bump(s, a)
        wait = tuple((s, v) for (s, v) in wait if v > 0)
        inc = tuple(inc)

        def run(E, fn=fn, wait=wait, inc=inc, S=self.S):
            for s, v in wait:
                E.wait_ge(S[s], v)
            ins = fn(E)
            for s, a in inc:
                ins.then_inc(S[s], a)
        self.items[eng].append(run)

    def wait(self, eng, s, v):
        if v > 0:
            self.items[eng].append(lambda E, s=s, v=v, S=self.S: E.wait_ge(S[s], v))

    def svop(self, eng, fn, wait=()):
        sem = "sv_" + eng
        self.op(eng, fn, wait=wait, inc=[(sem, 1)])
        self.wait(eng, sem, self.cur(sem))


PHASES = ("setup", "ln", "tp", "cls", "route", "attn", "wout", "ffn", "final")


def build_program(upto="final"):
    lim = PHASES.index(upto)

    def cut(name):
        return PHASES.index(name) > lim
    nc = bass.Bass("TRN2", target_bir_lowering=False)
    p = Plan()

    x_in = nc.dram_tensor("x", [TH, C], F32, kind="ExternalInput")
    wrvk = nc.dram_tensor("wrvk", [G, 3, 128, 2048], F32, kind="ExternalInput")
    wout_d = nc.dram_tensor("wout", [16, 128, 2048], F32, kind="ExternalInput")
    wcls_d = nc.dram_tensor("wcls", [128, 192], F32, kind="ExternalInput")
    wgu = nc.dram_tensor("wgu", [NE, 16, 128, 2048], F32, kind="ExternalInput")
    wdn = nc.dram_tensor("wdn", [NE, 8, 128, 2048], F32, kind="ExternalInput")
    td_d = nc.dram_tensor("td", [128, 16], F32, kind="ExternalInput")
    tf_d = nc.dram_tensor("tf", [128, 16], F32, kind="ExternalInput")
    brep_d = nc.dram_tensor("brep", [128, 12], F32, kind="ExternalInput")
    ident_d = nc.dram_tensor("ident", [128, 128], F32, kind="ExternalInput")
    mask_d = nc.dram_tensor("mask", [128, HALO], F32, kind="ExternalInput")
    out_d = nc.dram_tensor("out", [TO, C], F32, kind="ExternalOutput")
    if DBG:
        dbg_ar = nc.dram_tensor("dbg_ar", [128, 9472], F32, kind="ExternalOutput")
        dbg_y = nc.dram_tensor("dbg_y", [128, 16384], BF16, kind="ExternalOutput")
        dbg_xn = nc.dram_tensor("dbg_xn", [128, G * TH], BF16, kind="ExternalOutput")
        dbg_mr = nc.dram_tensor("dbg_mr", [128, 2 * NTG], F32, kind="ExternalOutput")
        dbg_rt = nc.dram_tensor("dbg_rt", [128, 96], F32, kind="ExternalOutput")
        dbg_acc = nc.dram_tensor("dbg_acc", [128, 16384], F32,
                                 kind="ExternalOutput")
        dbg_sk = nc.dram_tensor("dbg_sk", [128, NOG], F32, kind="ExternalOutput")
        dbg_rb = nc.dram_tensor("dbg_rb", [128, 384], F32, kind="ExternalOutput")
        dbg_lg = nc.dram_tensor("dbg_lg", [12, 1024], F32, kind="ExternalOutput")
        dbg_pr = nc.dram_tensor("dbg_pr", [128, 96], F32, kind="ExternalOutput")
        dbg_act = nc.dram_tensor("dbg_act", [128, 8192], BF16, kind="ExternalOutput")
        dbg_w8b = nc.dram_tensor("dbg_w8b", [128, 3072], F32, kind="ExternalOutput")
    schi = nc.dram_tensor("schi", [TH, C], BF16)
    w8dram = nc.dram_tensor("w8dram", [12, 1024], F32)
    sclo = nc.dram_tensor("sclo", [TO, C], BF16)

    ctx = ExitStack()
    E = ctx.enter_context

    ident = E(nc.sbuf_tensor("identsb", [128, 128], F32))
    td = E(nc.sbuf_tensor("tdsb", [128, 16], F32))
    tf = E(nc.sbuf_tensor("tfsb", [128, 16], F32))
    lam = E(nc.sbuf_tensor("lamsb", [128, 16], F32))
    eu = E(nc.sbuf_tensor("eusb", [128, 16], F32))
    brep = E(nc.sbuf_tensor("brepsb", [128, 12], F32))
    mask = E(nc.sbuf_tensor("masksb", [128, HALO], F32))
    wcls = E(nc.sbuf_tensor("wclssb", [128, 192], F32))
    clsh = E(nc.sbuf_tensor("clsh", [128, 192], BF16))
    clsl = E(nc.sbuf_tensor("clsl", [128, 192], BF16))
    ones = E(nc.sbuf_tensor("onesrow", [1, 128], F32))
    sumc = E(nc.sbuf_tensor("sumc", [128, NTG], F32))
    sqc = E(nc.sbuf_tensor("sqc", [128, NTG], F32))
    meanc = E(nc.sbuf_tensor("meanc", [128, NTG], F32))
    rstdc = E(nc.sbuf_tensor("rstdc", [128, NTG], F32))
    skipc = E(nc.sbuf_tensor("skipc", [128, NOG], F32))
    tmpa = E(nc.sbuf_tensor("tmpa", [128, 1], F32))
    tmpb = E(nc.sbuf_tensor("tmpb", [128, 1], F32))
    tmpc = E(nc.sbuf_tensor("tmpc", [128, 1], F32))
    epsc = E(nc.sbuf_tensor("epsc", [128, 1], F32))
    lgb = E(nc.sbuf_tensor("lgb", [128, 96], F32))
    eab = E(nc.sbuf_tensor("eab", [128, 96], F32))
    prb = E(nc.sbuf_tensor("prb", [128, 96], F32))
    w12 = E(nc.sbuf_tensor("w12", [128, 96], F32))
    mxc = E(nc.sbuf_tensor("mxc", [128, 8], F32))
    inv8 = E(nc.sbuf_tensor("inv8", [128, 8], F32))
    afin = E(nc.sbuf_tensor("afin", [128, 8], F32))
    bfin = E(nc.sbuf_tensor("bfin", [128, 8], F32))
    xnfm = E(nc.sbuf_tensor("xnfm", [128, G * TH], BF16))
    WS = [E(nc.sbuf_tensor(f"ws{i}", [128, 2048], BF16)) for i in range(RING)]
    AR = E(nc.sbuf_tensor("ar32", [128, 20480], F32))
    AB = E(nc.sbuf_tensor("arb16", [128, 16384], BF16))
    PS = [E(nc.psum_tensor(f"ps{i}", [128, 512], F32)) for i in range(8)]

    def ar(a, b):
        return AR[:, a:b]
    ACCo = 0
    XT = [(0, 2048), (2048, 4096)]
    XNF = [(6144, 8192), (8192, 10240)]
    LOG = 16384
    W8F = 19456
    EKo, Vo, Ao, Po, Qo = 0, 1280, 2560, 3840, 5120
    SRo, NUMo, DENo = 6400, 7424, 8448
    EK2o, V2o, SR2o = 9472, 10752, 12032
    SIL = [(16384, 16896), (16896, 17408)]
    W8B = [(17408, 18432), (18432, 19456)]
    XTF = [(16384, 18432), (18432, 20480)]

    def acc_ap(tg, a, b):
        return AR[:, 2048 * tg + a:2048 * tg + b]

    def xnh_ap(s):
        return AB[:, 2048 * s:2048 * s + 2048]

    def xnl_ap(s):
        return AB[:, 4096 + 2048 * s:4096 + 2048 * s + 2048]

    def xlr_ap(s, a, b):
        return AB[:, 1024 * s + a:1024 * s + b]

    def y_ap(kc, a, b):
        return AB[:, 1024 * kc + a:1024 * kc + b]

    def act_ap(ig, a, b):
        return AB[:, 1024 * ig + a:1024 * ig + b]

    def xnfm_ap(kc, a, b):
        return xnfm[:, TH * kc + a:TH * kc + b]

    sem_names = (["ld", "sx", "ssc", "stp", "sxl", "sout", "wfree",
                  "pe", "act", "dve", "pool", "w8b", "sv_vector", "sv_scalar"]
                 + [f"ws{i}" for i in range(RING)])
    for s in sem_names:
        p.S[s] = E(nc.semaphore(s))

    bank_free = [None] * 8

    def pe_use_banks(banks):
        w = []
        for b in banks:
            if bank_free[b] is not None:
                w.append(bank_free[b])
                bank_free[b] = None
        best = {}
        for s, v in w:
            best[s] = max(best.get(s, 0), v)
        return [(s, v) for s, v in best.items()]

    def banks_freed_by(banks, sem, val):
        for b in banks:
            bank_free[b] = (sem, val)

    slab_srcs = []
    for g in range(G):
        for s in range(3):
            slab_srcs.append(wrvk[g, s])
    for kc in range(16):
        slab_srcs.append(wout_d[kc])
    for e in range(NE):
        for kc in range(16):
            slab_srcs.append(wgu[e, kc])
        for kc in range(8):
            slab_srcs.append(wdn[e, kc])
    assert len(slab_srcs) == 256

    def emit_weight_head():
        p.wait("gpsimd", "ld", 96)
        for j in range(min(len(slab_srcs), RING)):
            p.op("gpsimd",
                 lambda E, d=WS[j % RING], s=slab_srcs[j]:
                     E.dma_start(d[:, :], s[:, :]),
                 inc=[(f"ws{j % RING}", 16)])

    ws_done = [False]

    def emit_weight_stream():
        if ws_done[0]:
            return
        ws_done[0] = True
        nfree = p.cur("wfree")
        nslab = min(len(slab_srcs), nfree + RING)
        for j in range(RING, nslab):
            p.wait("gpsimd", "wfree", j - RING + 1)
            p.op("gpsimd",
                 lambda E, d=WS[j % RING], s=slab_srcs[j]:
                     E.dma_start(d[:, :], s[:, :]),
                 inc=[(f"ws{j % RING}", 16)])

    def slab_ready_wait(j):
        return (f"ws{j % RING}", 16 * (j // RING + 1))

    for dst, src in ((ident, ident_d), (td, td_d), (tf, tf_d),
                     (brep, brep_d), (mask, mask_d), (wcls, wcls_d)):
        p.op("sync", lambda E, d=dst, s=src: E.dma_start(d[:, :], s[:, :]),
             inc=[("ld", 16)])

    p.svop("scalar", lambda E: E.activation(lam[:, :], td[:, :], AF.Exp),
           wait=[("ld", 96)])
    p.op("scalar", lambda E: E.activation(lam[:, :], lam[:, :], AF.Exp, scale=-1.0))
    p.op("scalar", lambda E: E.activation(eu[:, :], tf[:, :], AF.Exp))
    p.op("scalar", lambda E: E.activation(clsh[:, :], wcls[:, :], AF.Copy),
         inc=[("act", 1)])
    act_setup = p.cur("act")
    p.op("vector", lambda E: E.memset(ones[:, :], 1.0))
    p.op("vector", lambda E: E.memset(epsc[:, :], EPS))
    p.op("vector",
         lambda E: E.tensor_tensor(clsl[:, :], wcls[:, :], clsh[:, :], ALU.subtract),
         wait=[("act", act_setup)], inc=[("dve", 1)])
    dve_setup = p.cur("dve")

    emit_weight_head()
    if cut("ln"):
        return finish(nc, p, ctx, emit_weight_stream)
    xt_free = [0, 0]
    xnh_rd = [0, 0]
    xnl_wr = [0, 0]
    a3 = [0, 0]
    d3 = [0, 0]

    for tg in range(NTG):
        sl = tg % 2
        p.op("sync",
             lambda E, d=ar(*XT[sl]), s=x_in[128 * tg:128 * tg + 128, :]:
                 E.dma_start(d, s),
             wait=[("dve", xt_free[sl])], inc=[("sx", 16)])
        p.op("scalar",
             lambda E, x=ar(*XT[sl]), o=ar(*XNF[sl]), a=sumc[:, tg:tg + 1]:
                 E.activation(o, x, AF.Copy, accum_out=a),
             wait=[("sx", p.cur("sx")), ("dve", d3[sl]),
                   ("pe", xnh_rd[sl])])
        p.op("scalar",
             lambda E, x=ar(*XT[sl]), o=ar(*XNF[sl]), a=sqc[:, tg:tg + 1]:
                 E.activation(o, x, AF.Square, accum_out=a),
             inc=[("act", 1)])
        a1 = p.cur("act")
        p.svop("vector",
               lambda E, o=meanc[:, tg:tg + 1], i=sumc[:, tg:tg + 1]:
                   E.tensor_scalar(o, i, 1.0 / C, None, ALU.mult),
               wait=[("act", a1)])
        p.svop("vector",
               lambda E, i=sqc[:, tg:tg + 1]:
                   E.tensor_scalar(tmpa[:, :], i, 1.0 / C, None, ALU.mult))
        p.svop("vector",
               lambda E, m=meanc[:, tg:tg + 1]:
                   E.tensor_tensor(tmpb[:, :], m, m, ALU.mult))
        p.op("vector",
             lambda E: E.tensor_tensor(tmpc[:, :], tmpa[:, :], tmpb[:, :],
                                       ALU.subtract),
             inc=[("dve", 1)])
        d1 = p.cur("dve")
        p.svop("scalar",
               lambda E: E.activation(tmpc[:, :], tmpc[:, :], AF.Ln,
                                      bias=epsc[:, 0:1]),
               wait=[("dve", d1)])
        p.op("scalar",
             lambda E, o=rstdc[:, tg:tg + 1]:
                 E.activation(o, tmpc[:, :], AF.Exp, scale=-0.5),
             inc=[("act", 1)])
        a2 = p.cur("act")
        p.op("vector",
             lambda E, o=ar(*XNF[sl]), x=ar(*XT[sl]), m=meanc[:, tg:tg + 1],
                    r=rstdc[:, tg:tg + 1]:
                 E.tensor_scalar(o, x, m, r, ALU.subtract, ALU.mult),
             wait=[("act", a2)],
             inc=[("dve", 1)])
        d2 = p.cur("dve")
        xt_free[sl] = d2
        p.op("vector",
             lambda E, o=xnh_ap(sl), i=ar(*XNF[sl]): E.tensor_copy(o, i),
             inc=[("pool", 1)])
        a3[sl] = p.cur("pool")
        if tg >= 2:
            p.op("vector",
                 lambda E, o=xnl_ap(sl), i=ar(*XNF[sl]), h=xnh_ap(sl):
                     E.tensor_tensor(o, i, h, ALU.subtract),
                 wait=[("pool", a3[sl]), ("ssc", xnl_wr[sl])], inc=[("dve", 1)])
            d3[sl] = p.cur("dve")
        tbanks = [0, 1, 2, 3] if tg % 2 == 0 else [4, 5, 6, 7]
        for bi, b in enumerate(tbanks):
            w = pe_use_banks([b])
            if bi == 0:
                w += [("dve", d2)]
            for t in range(4):
                g4 = 4 * bi + t
                xo = XNF[sl][0] + 128 * g4
                p.op("tensor",
                     lambda E, o=PS[b][:, 128 * t:128 * t + 128],
                            i=ar(xo, xo + 128):
                         E.transpose(o, i, ident[:, :]),
                     wait=w if t == 0 else (),
                     inc=[("pe", 1)] if t == 3 else [])
            pe_b = p.cur("pe")
            eng = "scalar" if bi < 2 else "vector"
            for t in range(4):
                g4 = 4 * bi + t
                if eng == "scalar":
                    fn = (lambda E, o=xnfm_ap(g4, 128 * tg, 128 * tg + 128),
                                 i=PS[b][:, 128 * t:128 * t + 128]:
                          E.activation(o, i, AF.Copy))
                else:
                    fn = (lambda E, o=xnfm_ap(g4, 128 * tg, 128 * tg + 128),
                                 i=PS[b][:, 128 * t:128 * t + 128]:
                          E.tensor_copy(o, i))
                p.op(eng, fn,
                     wait=[("pe", pe_b)] if t == 0 else (),
                     inc=[("act" if eng == "scalar" else "dve", 1)]
                         if t == 3 else [])
            banks_freed_by([b], "act" if eng == "scalar" else "dve",
                           p.cur("act" if eng == "scalar" else "dve"))
        xnh_rd[sl] = p.cur("pe")
        if tg >= 2:
            p.op("gpsimd",
                 lambda E, d=sclo[128 * (tg - 2):128 * (tg - 2) + 128, :],
                        s=xnl_ap(sl):
                     E.dma_start(d, s),
                 wait=[("dve", d3[sl])], inc=[("ssc", 16)])
            xnl_wr[sl] = p.cur("ssc")

    if cut("tp"):
        return finish(nc, p, ctx, emit_weight_stream)
    p.wait("sync", "ssc", p.cur("ssc"))
    tpa_done = p.cur("act")
    tpd_done = p.cur("dve")

    xlr_free = [0, 0, 0, 0]

    if cut("cls"):
        return finish(nc, p, ctx, emit_weight_stream)
    cls_banks = [0, 1]
    w0 = pe_use_banks(cls_banks)
    for g in range(G):
        p.op("sync",
             lambda E, d=xlr_ap(g % 4, 0, 1024),
                    s=sclo[0:TO, 128 * g:128 * g + 128]:
                 E.dma_start(d, s, transpose=True),
             wait=[("pe", xlr_free[g % 4])], inc=[("sxl", 16)])
        waits = [("sxl", p.cur("sxl"))]
        if g == 0:
            waits += [("act", max(act_setup, tpa_done)),
                      ("dve", max(dve_setup, tpd_done))]
            waits += w0
        for tc in range(2):
            ps = PS[cls_banks[tc]]
            hi = clsh[:, 12 * g:12 * g + 12]
            lo = clsl[:, 12 * g:12 * g + 12]
            xh = xnfm_ap(g, HALO + 512 * tc, HALO + 512 * tc + 512)
            xl = xlr_ap(g % 4, 512 * tc, 512 * tc + 512)
            p.op("tensor",
                 lambda E, ps=ps, w=hi, x=xh, st=(g == 0):
                     E.matmul(ps[0:12, :], w, x, start=st, stop=False),
                 wait=waits if tc == 0 else ())
            p.op("tensor",
                 lambda E, ps=ps, w=lo, x=xh:
                     E.matmul(ps[0:12, :], w, x, start=False, stop=False))
            inc = [("pe", 1)] if tc == 1 else []
            p.op("tensor",
                 lambda E, ps=ps, w=hi, x=xl, sp=(g == G - 1):
                     E.matmul(ps[0:12, :], w, x, start=False, stop=sp),
                 inc=inc)
        xlr_free[g % 4] = p.cur("pe")
    cls_done = p.cur("pe")
    p.op("scalar",
         lambda E, o=AR[0:12, LOG:LOG + 512]:
             E.activation(o, PS[0][0:12, :], AF.Copy),
         wait=[("pe", cls_done)])
    p.op("scalar",
         lambda E, o=AR[0:12, LOG + 512:LOG + 1024]:
             E.activation(o, PS[1][0:12, :], AF.Copy),
         inc=[("act", 1)])
    lg_evac = p.cur("act")
    banks_freed_by([0, 1], "act", lg_evac)

    w1 = pe_use_banks([2])
    for tg in range(NOG):
        inc = [("pe", 1)] if tg == NOG - 1 else []
        p.op("tensor",
             lambda E, o=PS[2][:, 12 * tg:12 * tg + 12],
                    i=AR[0:12, LOG + 128 * tg:LOG + 128 * tg + 128]:
                 E.transpose(o, i, ident[0:12, 0:12]),
             wait=(w1 + [("act", lg_evac)]) if tg == 0 else (), inc=inc)
    lgT_done = p.cur("pe")
    p.op("vector",
         lambda E: E.tensor_copy(lgb[:, :], PS[2][:, 0:96]),
         wait=[("pe", lgT_done)], inc=[("dve", 1)])
    banks_freed_by([2], "dve", p.cur("dve"))

    def b3(buf):
        return buf[:, :].rearrange("p (b t) -> p b t", t=12)

    def bc8(col8):
        return col8[:, :].unsqueeze(2).broadcast_to((128, 8, 12))

    p.svop("vector", lambda E: E.reduce_max(mxc[:, :], b3(lgb), axis=AX.X))
    p.op("vector",
         lambda E: E.tensor_scalar(mxc[:, :], mxc[:, :], -1.0, None, ALU.mult),
         inc=[("dve", 1)])
    d_mx = p.cur("dve")
    for tg in range(NOG):
        s = slice(12 * tg, 12 * tg + 12)
        inc = [("act", 1)] if tg == NOG - 1 else []
        p.op("scalar",
             lambda E, o=eab[:, s], i=lgb[:, s], b=mxc[:, tg:tg + 1]:
                 E.activation(o, i, AF.Exp, bias=b),
             wait=[("dve", d_mx)] if tg == 0 else (), inc=inc)
    a_exp = p.cur("act")
    p.svop("vector", lambda E: E.reduce_sum(mxc[:, :], b3(eab), axis=AX.X),
           wait=[("act", a_exp)])
    p.svop("vector", lambda E: E.reciprocal(inv8[:, :], mxc[:, :]))
    p.op("vector",
         lambda E: E.tensor_tensor(b3(prb), b3(eab), bc8(inv8), ALU.mult))
    p.op("vector",
         lambda E, bb=brep[:, :].unsqueeze(1).broadcast_to((128, 8, 12)):
             E.tensor_tensor(b3(lgb), b3(prb), bb, ALU.add))
    p.svop("vector", lambda E: E.reduce_max(mxc[:, :], b3(lgb), axis=AX.X))
    p.op("vector",
         lambda E: E.tensor_tensor(b3(w12), b3(lgb), bc8(mxc), ALU.is_ge))
    p.op("vector",
         lambda E: E.scalar_tensor_tensor(b3(w12), b3(w12), NEG, b3(lgb),
                                          ALU.mult, ALU.add))
    p.svop("vector", lambda E: E.reduce_max(inv8[:, :], b3(w12), axis=AX.X))
    p.op("vector",
         lambda E: E.tensor_tensor(b3(w12), b3(lgb), bc8(inv8), ALU.is_ge))
    p.op("vector",
         lambda E: E.scalar_tensor_tensor(b3(w12), b3(prb), 1.5, b3(w12),
                                          ALU.mult, ALU.mult))
    p.svop("vector",
           lambda E, i=b3(w12)[:, :, 8:12]:
               E.reduce_sum(skipc[:, :], i, axis=AX.X))
    p.svop("vector",
           lambda E: E.tensor_tensor(afin[:, :], rstdc[:, 2:10], skipc[:, :],
                                     ALU.mult))
    p.svop("vector",
           lambda E: E.tensor_tensor(bfin[:, :], meanc[:, 2:10], afin[:, :],
                                     ALU.mult))
    p.svop("vector",
           lambda E: E.tensor_scalar(bfin[:, :], bfin[:, :], -1.0, None,
                                     ALU.mult))
    p.svop("vector",
           lambda E: E.tensor_scalar(afin[:, :], afin[:, :], 1.0, None,
                                     ALU.add))
    p.op("vector", lambda E: E.tensor_copy(tmpa[:, :], tmpa[:, :]),
         inc=[("dve", 1)])
    d_rt = p.cur("dve")

    if cut("attn"):
        return finish(nc, p, ctx, emit_weight_stream)
    FULL_CH = [(0, 512), (512, 1024), (1024, 1280)]
    OWN_CH = [(HALO, HALO + 512), (HALO + 512, TH)]
    lamap = {g: lam[:, g:g + 1].broadcast_to((128, TH)) for g in range(G)}
    wkv_done = 0
    wkv_dones = []
    sub_banks = {"r": [0, 1], "v": [2, 3, 4], "k": [5, 6, 7]}

    for g in range(G):
        eko = EKo if g % 2 == 0 else EK2o
        vo = Vo if g % 2 == 0 else V2o
        sro = SRo if g % 2 == 0 else SR2o
        evac_acts = {}
        for si, (sname, chunks) in enumerate(
                (("r", OWN_CH), ("v", FULL_CH), ("k", FULL_CH))):
            j = 3 * g + si
            banks = sub_banks[sname]
            waits = [slab_ready_wait(j)] + pe_use_banks(banks)
            if g == 0 and si == 0:
                waits += [("act", tpa_done), ("dve", tpd_done)]
            wslab = WS[j % RING]
            nkc = 16
            for kc in range(nkc):
                wt = wslab[:, 128 * kc:128 * kc + 128]
                for ci, (c0, c1) in enumerate(chunks):
                    last = (kc == nkc - 1 and ci == len(chunks) - 1)
                    inc = ([("pe", 1)] if last else [])
                    p.op("tensor",
                         lambda E, ps=PS[banks[ci]], w=wt,
                                x=xnfm_ap(kc, c0, c1), st=(kc == 0),
                                sp=(kc == nkc - 1), n=c1 - c0:
                             E.matmul(ps[:, 0:n], w, x, start=st, stop=sp),
                         wait=waits if kc == 0 and ci == 0 else (), inc=inc)
            evac_acts[sname] = p.cur("pe")

        war = [("dve", wkv_dones[g - 2] if g >= 2 else 0)]
        p.op("scalar",
             lambda E, o=ar(sro, sro + 512), i=PS[0][:, :]:
                 E.activation(o, i, AF.Sigmoid),
             wait=[("pe", evac_acts["r"])] + war, inc=[("wfree", 1)])
        p.op("scalar",
             lambda E, o=ar(sro + 512, sro + 1024), i=PS[1][:, :]:
                 E.activation(o, i, AF.Sigmoid),
             inc=[("act", 1)])
        banks_freed_by([0, 1], "act", p.cur("act"))
        p.wait("scalar", "pe", evac_acts["v"])
        for ci, (c0, c1) in enumerate(FULL_CH):
            inc = [("wfree", 1)] if ci == 0 else ([("act", 1)] if ci == 2 else [])
            p.op("scalar",
                 lambda E, o=ar(vo + c0, vo + c1), i=PS[2 + ci][:, 0:c1 - c0]:
                     E.activation(o, i, AF.Copy),
                 inc=inc)
        banks_freed_by([2, 3, 4], "act", p.cur("act"))
        p.wait("scalar", "pe", evac_acts["k"])
        for ci, (c0, c1) in enumerate(FULL_CH):
            inc = [("wfree", 1)] if ci == 0 else ([("act", 1)] if ci == 2 else [])
            p.op("scalar",
                 lambda E, o=ar(eko + c0, eko + c1), i=PS[5 + ci][:, 0:c1 - c0]:
                     E.activation(o, i, AF.Exp),
                 inc=inc)
        banks_freed_by([5, 6, 7], "act", p.cur("act"))
        a_ev = p.cur("act")

        if g == G - 1:
            w2 = pe_use_banks([3, 4])
            for tg8 in range(NOG):
                b = 3 + tg8 // 4
                inc = [("pe", 1)] if tg8 == NOG - 1 else []
                p.op("tensor",
                     lambda E, o=PS[b][0:12,
                                       128 * (tg8 % 4):128 * (tg8 % 4) + 128],
                            i=w12[:, 12 * tg8:12 * tg8 + 12]:
                         E.transpose(o, i, ident[:, :]),
                     wait=(w2 + [("dve", d_rt)]) if tg8 == 0 else (), inc=inc)
            w8T_done = p.cur("pe")
            p.op("scalar",
                 lambda E, o=AR[0:12, W8F:W8F + 512]:
                     E.activation(o, PS[3][0:12, :], AF.Copy),
                 wait=[("pe", w8T_done)])
            p.op("scalar",
                 lambda E, o=AR[0:12, W8F + 512:W8F + 1024]:
                     E.activation(o, PS[4][0:12, :], AF.Copy),
                 inc=[("act", 1)])
            w8fm_ready = p.cur("act")
            banks_freed_by([3, 4], "act", w8fm_ready)
            p.op("sync",
                 lambda E, s=AR[0:12, W8F:W8F + 1024]:
                     E.dma_start(w8dram[:, :], s),
                 wait=[("act", w8fm_ready)], inc=[("w8b", 16)])

        p.op("vector",
             lambda E, ek=ar(eko, eko + HALO):
                 E.tensor_tensor(ek, ek, mask[:, :], ALU.mult),
             wait=[("act", a_ev)])
        p.op("vector",
             lambda E, o=ar(Ao, Ao + TH), ek=ar(eko, eko + TH),
                    v=ar(vo, vo + TH):
                 E.tensor_tensor(o, ek, v, ALU.mult))
        p.svop("vector",
               lambda E, o=ar(Po, Po + TH), l=lamap[g], a=ar(Ao, Ao + TH):
                   E.tensor_tensor_scan(o, l, a, 0.0, ALU.mult, ALU.add))
        p.svop("vector",
               lambda E, o=ar(Qo, Qo + TH), l=lamap[g], a=ar(eko, eko + TH):
                   E.tensor_tensor_scan(o, l, a, 0.0, ALU.mult, ALU.add))
        p.op("vector",
             lambda E, o=ar(NUMo, NUMo + TO), a=ar(Ao + HALO, Ao + TH),
                    u=eu[:, g:g + 1], ps=ar(Po + HALO - 1, Po + TH - 1):
                 E.scalar_tensor_tensor(o, a, u, ps, ALU.mult, ALU.add))
        p.op("vector",
             lambda E, o=ar(DENo, DENo + TO), a=ar(eko + HALO, eko + TH),
                    u=eu[:, g:g + 1], qs=ar(Qo + HALO - 1, Qo + TH - 1):
                 E.scalar_tensor_tensor(o, a, u, qs, ALU.mult, ALU.add),
             inc=[("dve", 1)])
        d_den = p.cur("dve")
        p.op("scalar",
             lambda E, d=ar(DENo, DENo + TO): E.activation(d, d, AF.Ln),
             wait=[("dve", d_den)])
        p.op("scalar",
             lambda E, d=ar(DENo, DENo + TO):
                 E.activation(d, d, AF.Exp, scale=-1.0),
             inc=[("act", 1)])
        a_inv = p.cur("act")
        p.op("vector",
             lambda E, o=ar(vo, vo + TO), n=ar(NUMo, NUMo + TO),
                    i=ar(DENo, DENo + TO):
                 E.tensor_tensor(o, n, i, ALU.mult),
             wait=[("act", a_inv)])
        p.op("vector",
             lambda E, o=y_ap(g, 0, TO), s=ar(sro, sro + TO),
                    w=ar(vo, vo + TO):
                 E.tensor_tensor(o, s, w, ALU.mult),
             inc=[("dve", 1)])
        wkv_done = p.cur("dve")
        wkv_dones.append(wkv_done)

    if DBG:
        p.wait("sync", "dve", wkv_done)
        p.op("sync", lambda E: E.dma_start(dbg_ar[:, :], AR[:, 0:9472]),
             inc=[("sout", 16)])
        p.op("sync", lambda E: E.dma_start(dbg_y[:, :], AB[:, 0:16384]),
             inc=[("sout", 16)])
        p.op("sync", lambda E: E.dma_start(dbg_xn[:, :], xnfm[:, :]),
             inc=[("sout", 16)])
        p.op("sync", lambda E, s=meanc: E.dma_start(dbg_mr[:, 0:NTG], s[:, :]),
             inc=[("sout", 16)])
        p.op("sync", lambda E, s=rstdc: E.dma_start(dbg_mr[:, NTG:2 * NTG], s[:, :]),
             inc=[("sout", 16)])
        p.op("sync", lambda E, s=w12: E.dma_start(dbg_rt[:, :], s[:, :]),
             inc=[("sout", 16)])
        p.wait("sync", "sout", p.cur("sout"))
    def dump_acc(last_dve):
        p.wait("sync", "dve", last_dve)
        p.op("sync", lambda E: E.dma_start(dbg_acc[:, :], AR[:, 0:16384]),
             inc=[("sout", 16)])
        p.op("sync", lambda E, s=w12: E.dma_start(dbg_rt[:, :], s[:, :]),
             inc=[("sout", 16)])
        p.op("sync", lambda E, s=skipc: E.dma_start(dbg_sk[:, :], s[:, :]),
             inc=[("sout", 16)])
        for bi, buf in enumerate((lgb, eab, prb, w12)):
            p.op("sync",
                 lambda E, d=dbg_rb, bi=bi, s=buf:
                     E.dma_start(d[:, 96 * bi:96 * bi + 96], s[:, :]),
                 inc=[("sout", 16)])
        p.op("sync",
             lambda E, s=AR[0:12, LOG:LOG + 1024]: E.dma_start(dbg_lg[:, :], s),
             inc=[("sout", 16)])
        p.op("sync", lambda E, s=AB[:, 0:8192]: E.dma_start(dbg_act[:, :], s),
             inc=[("sout", 16)])
        p.op("sync",
             lambda E, s=AR[:, 17408:20480]: E.dma_start(dbg_w8b[:, :], s),
             inc=[("sout", 16)])
        p.op("sync", lambda E, s=prb: E.dma_start(dbg_pr[:, :], s[:, :]),
             inc=[("sout", 16)])
        p.wait("sync", "sout", p.cur("sout"))

    if cut("wout"):
        return finish(nc, p, ctx, emit_weight_stream)
    acc_init_done = [0] * NOG
    acc_last = [0] * NOG
    for half in range(2):
        kcs = list(range(8 * half, 8 * half + 8))
        jbase = 48
        for tg in range(NOG):
            par = tg % 2
            banks = [4 * par + h for h in range(4)]
            waits = pe_use_banks(banks)
            if tg == 0:
                waits += [slab_ready_wait(jbase + kc) for kc in kcs]
                waits += [("dve", wkv_dones[kcs[-1]])]
            first = True
            for ki, kc in enumerate(kcs):
                wslab = WS[(jbase + kc) % RING]
                for hc in range(4):
                    last = (ki == 7 and hc == 3)
                    if last and tg == NOG - 1:
                        inc = [("wfree", 8)]
                    elif last:
                        inc = [("pe", 1)]
                    else:
                        inc = []
                    p.op("tensor",
                         lambda E, ps=PS[banks[hc]],
                                w=y_ap(kc, 128 * tg, 128 * tg + 128),
                                x=wslab[:, 512 * hc:512 * hc + 512],
                                st=(ki == 0), sp=(ki == 7):
                             E.matmul(ps[:, :], w, x, start=st, stop=sp),
                         wait=waits if first else (), inc=inc)
                    first = False
            pe_tg = p.cur("pe")
            ev_wait = ("pe", pe_tg) if tg != NOG - 1 else ("wfree", p.cur("wfree"))
            for hc in range(4):
                fn_copy = (half == 0)
                inc = [("dve", 1)] if hc == 3 else []
                if fn_copy:
                    p.op("vector",
                         lambda E, o=acc_ap(tg, 512 * hc, 512 * hc + 512),
                                i=PS[banks[hc]]:
                             E.tensor_copy(o, i[:, :]),
                         wait=[ev_wait] if hc == 0 else (), inc=inc)
                else:
                    p.op("vector",
                         lambda E, o=acc_ap(tg, 512 * hc, 512 * hc + 512),
                                i=PS[banks[hc]]:
                             E.tensor_tensor(o, o, i[:, :], ALU.add),
                         wait=[ev_wait] if hc == 0 else (), inc=inc)
            banks_freed_by(banks, "dve", p.cur("dve"))
            if half == 0:
                acc_init_done[tg] = p.cur("dve")
            else:
                acc_last[tg] = p.cur("dve")

    y_free_pe = p.cur("pe")

    if DBG and cut("ffn"):
        dump_acc(max(acc_last))
    if cut("ffn"):
        return finish(nc, p, ctx, emit_weight_stream)
    fin_state = {"xtf_free": {}, "d_outs": {}, "emitted": set()}

    def emit_final(tg, xslots, slot_of):
        sl = slot_of[tg % 4]
        xs = xslots[sl]
        p.op("sync",
             lambda E, d=xs,
                    s=x_in[HALO + 128 * tg:HALO + 128 * tg + 128, :]:
                 E.dma_start(d, s),
             wait=[fin_state["xtf_free"][sl]], inc=[("sx", 16)])
        sx_v = p.cur("sx")
        p.op("scalar",
             lambda E, x=xs, a=afin[:, tg:tg + 1], b=bfin[:, tg:tg + 1]:
                 E.activation(x, x, AF.Identity, bias=b, scale=a),
             wait=[("sx", sx_v), ("dve", d_rt)],
             inc=[("act", 1)])
        a_fin = p.cur("act")
        p.op("vector",
             lambda E, o=acc_ap(tg, 0, 2048), x=xs:
                 E.tensor_tensor(o, o, x, ALU.add),
             wait=[("act", a_fin), ("dve", acc_last[tg])], inc=[("dve", 1)])
        d_out = p.cur("dve")
        fin_state["xtf_free"][sl] = ("dve", d_out)
        fin_state["d_outs"][tg] = d_out
        fin_state["emitted"].add(tg)

    prev_down_war = 0
    prev_act_war = 0
    w8bc_free = [0, 0]
    for e in range(min(NE, FFN_NE)):
        ep = e % 2
        p.op("sync",
             lambda E, d=ar(W8B[ep][0], W8B[ep][0] + 1024),
                    s=w8dram[e:e + 1, :].broadcast_to((128, 1024)):
                 E.dma_start(d, s),
             wait=[("w8b", 16), ("dve", w8bc_free[ep])], inc=[("w8b", 16)])
        w8bc_ready = p.cur("w8b")

        if FFN_PART < 2:
            continue
        silu_done = {}
        gu_pe = {}
        for ig in range(NIG):
            Q, igp = ig // 2, ig % 2
            par = ig % 2
            banks = [4 * par, 4 * par + 1, 4 * par + 2, 4 * par + 3]
            waits = pe_use_banks(banks)
            first = True
            for kc in range(16):
                j = 64 + 24 * e + 4 * Q + (kc % 4)
                if igp == 0 and kc < 4:
                    waits2 = [slab_ready_wait(j)]
                else:
                    waits2 = []
                wslab = WS[j % RING]
                cb = (kc // 4) * 512 + igp * 256
                gt = wslab[:, cb:cb + 128]
                ut = wslab[:, cb + 128:cb + 256]
                for tc in range(2):
                    xr = xnfm_ap(kc, HALO + 512 * tc, HALO + 512 * tc + 512)
                    p.op("tensor",
                         lambda E, ps=PS[banks[tc]], w=gt, x=xr,
                                st=(kc == 0), sp=(kc == 15):
                             E.matmul(ps[:, :], w, x, start=st, stop=sp),
                         wait=(waits + waits2) if first else waits2)
                    first = False
                    waits2 = []
                for tc in range(2):
                    xr = xnfm_ap(kc, HALO + 512 * tc, HALO + 512 * tc + 512)
                    last = (kc == 15 and tc == 1)
                    inc = [("pe", 1)] if last else []
                    p.op("tensor",
                         lambda E, ps=PS[banks[2 + tc]], w=ut, x=xr,
                                st=(kc == 0), sp=(kc == 15):
                             E.matmul(ps[:, :], w, x, start=st, stop=sp),
                         inc=inc)
            gu_pe[ig] = p.cur("pe")
            if FFN_PART < 3:
                banks_freed_by(banks, "pe", gu_pe[ig])
                silu_done[ig] = p.cur("dve")
                if igp == 1:
                    p.op("vector",
                         lambda E: E.tensor_copy(tmpa[:, :], tmpa[:, :]),
                         wait=[("pe", gu_pe[ig])], inc=[("wfree", 4)])
                continue
            sil_war = silu_done.get(ig - 1, prev_act_war)
            p.op("scalar",
                 lambda E, o=ar(*SIL[0]), i=PS[banks[0]]:
                     E.activation(o, i[:, :], AF.Silu),
                 wait=[("pe", gu_pe[ig]), ("dve", sil_war)],
                 inc=[("wfree", 4)] if igp == 1 else ())
            p.op("scalar",
                 lambda E, o=ar(*SIL[1]), i=PS[banks[1]]:
                     E.activation(o, i[:, :], AF.Silu),
                 inc=[("act", 1)])
            a_sil = p.cur("act")
            banks_freed_by(banks[0:2], "act", a_sil)
            if FFN_PART < 4:
                banks_freed_by(banks[2:4], "act", a_sil)
                silu_done[ig] = p.cur("dve")
                continue
            wv = [("act", a_sil), ("wfree", prev_down_war)]
            if ig == 0:
                wv.append(("w8b", w8bc_ready))
            for tc in range(2):
                p.op("vector",
                     lambda E, o=ar(*SIL[tc]), i=PS[banks[2 + tc]]:
                         E.tensor_tensor(o, o, i[:, :], ALU.mult),
                     wait=wv if tc == 0 else ())
                p.op("vector",
                     lambda E, o=act_ap(ig, 512 * tc, 512 * tc + 512),
                            a=ar(*SIL[tc]),
                            w=ar(W8B[ep][0] + 512 * tc,
                                 W8B[ep][0] + 512 * tc + 512):
                         E.tensor_tensor(o, a, w, ALU.mult),
                     inc=[("dve", 1)] if tc == 1 else [])
            silu_done[ig] = p.cur("dve")
            banks_freed_by(banks[2:4], "dve", silu_done[ig])
        w8bc_free[ep] = silu_done[NIG - 1]
        prev_act_war = silu_done[NIG - 1]
        acts_ready = silu_done[NIG - 1]

        if e == NE - 1 and FFN_PART >= 5 and not cut("final"):
            emit_weight_stream()
            xslots_e = [None, None,
                        AB[:, 8192:12288].bitcast(F32),
                        AB[:, 12288:16384].bitcast(F32)]
            slot_of_e = [2, 3, 2, 3]
            fin_state["xtf_free"][2] = ("pe", y_free_pe)
            fin_state["xtf_free"][3] = ("pe", y_free_pe)
            for tg in range(4):
                emit_final(tg, xslots_e, slot_of_e)

        if FFN_PART < 5:
            continue
        for tg in range(NOG):
            par = tg % 2
            banks = [4 * par + h for h in range(4)]
            waits = pe_use_banks(banks)
            if tg == 0:
                waits += [slab_ready_wait(64 + 24 * e + 16 + kc)
                          for kc in range(8)]
            first = True
            for kc in range(8):
                j = 64 + 24 * e + 16 + kc
                wslab = WS[j % RING]
                at = act_ap(kc, 128 * tg, 128 * tg + 128)
                if tg == 0:
                    waits = waits + [("dve", silu_done[kc])] if first else \
                        [("dve", silu_done[kc])]
                for hc in range(4):
                    last = (kc == 7 and hc == 3)
                    if last and tg == NOG - 1:
                        inc = [("wfree", 8)]
                    elif last:
                        inc = [("pe", 1)]
                    else:
                        inc = []
                    p.op("tensor",
                         lambda E, ps=PS[banks[hc]], w=at,
                                x=wslab[:, 512 * hc:512 * hc + 512],
                                st=(kc == 0), sp=(kc == 7):
                             E.matmul(ps[:, :], w, x, start=st, stop=sp),
                         wait=waits if (first or (tg == 0 and hc == 0)) else (),
                         inc=inc)
                    if tg == 0 and hc == 3:
                        waits = []
                    first = False
            pe_tg = p.cur("pe")
            ev_wait = ("pe", pe_tg) if tg != NOG - 1 else ("wfree", p.cur("wfree"))
            if tg == NOG - 1:
                prev_down_war = p.cur("wfree")
            for hc in range(4):
                inc = [("dve", 1)] if hc == 3 else []
                p.op("vector",
                     lambda E, o=acc_ap(tg, 512 * hc, 512 * hc + 512),
                            i=PS[banks[hc]]:
                         E.tensor_tensor(o, o, i[:, :], ALU.add),
                     wait=[ev_wait] if hc == 0 else (),
                     inc=inc)
            banks_freed_by(banks, "dve", p.cur("dve"))
            acc_last[tg] = p.cur("dve")

    if DBG and cut("final"):
        dump_acc(max(acc_last))
    if cut("final"):
        return finish(nc, p, ctx, emit_weight_stream)
    emit_weight_stream()
    xslots = [ar(*XTF[0]), ar(*XTF[1]),
              AB[:, 8192:12288].bitcast(F32), AB[:, 12288:16384].bitcast(F32)]
    slot_of = [2, 3, 0, 1]
    for sl, ev in ((0, ("dve", prev_act_war)), (1, ("dve", prev_act_war))):
        fin_state["xtf_free"].setdefault(sl, ev)
    fin_state["xtf_free"].setdefault(2, ("pe", y_free_pe))
    fin_state["xtf_free"].setdefault(3, ("pe", y_free_pe))
    for tg in range(NOG):
        if tg in fin_state["emitted"]:
            continue
        emit_final(tg, xslots, slot_of)
    for tg in range(NOG):
        p.op("gpsimd",
             lambda E, d=out_d[128 * tg:128 * tg + 128, :],
                    s=acc_ap(tg, 0, 2048):
                 E.dma_start(d, s),
             wait=[("dve", fin_state["d_outs"][tg]), ("dve", acc_last[tg])],
             inc=[("sout", 16)])
    p.wait("gpsimd", "sout", p.cur("sout"))
    p.wait("scalar", "sout", p.cur("sout"))
    p.wait("sync", "sout", p.cur("sout"))
    return finish(nc, p, ctx, emit_weight_stream)


def finish(nc, p, ctx, emit_weight_stream):
    emit_weight_stream()
    with nc.Block() as block:
        @block.tensor
        def _(e):
            for f in p.items["tensor"]:
                f(e)

        @block.vector
        def _(e):
            for f in p.items["vector"]:
                f(e)

        @block.scalar
        def _(e):
            for f in p.items["scalar"]:
                f(e)

        @block.gpsimd
        def _(e):
            for f in p.items["gpsimd"]:
                f(e)

        @block.sync
        def _(e):
            for f in p.items["sync"]:
                f(e)

    ctx.close()
    return nc


_NC_CACHE = None


def _get_nc():
    global _NC_CACHE
    if _NC_CACHE is None:
        _NC_CACHE = build_program()
    return _NC_CACHE


def kernel(L0_attn_x, L0_attn_W_rvk, L0_attn_W_out, L0_attn_time_decay,
           L0_attn_time_first, L0_ffn_classifier_w,
           L0_ffn_e_score_correction_bias, L0_ffn_gate_w, L0_ffn_up_w,
           L0_ffn_down_w):
    x = np.ascontiguousarray(np.asarray(L0_attn_x, dtype=np.float32)
                             .reshape(8192, 2048))
    W_rvk = np.asarray(L0_attn_W_rvk, dtype=np.float32)
    W_out = np.asarray(L0_attn_W_out, dtype=np.float32)
    tdv = np.asarray(L0_attn_time_decay, dtype=np.float32)
    tfv = np.asarray(L0_attn_time_first, dtype=np.float32)
    clsw = np.asarray(L0_ffn_classifier_w, dtype=np.float32)
    biasv = np.asarray(L0_ffn_e_score_correction_bias, dtype=np.float32)
    gate = np.asarray(L0_ffn_gate_w, dtype=np.float32)
    up = np.asarray(L0_ffn_up_w, dtype=np.float32)
    down = np.asarray(L0_ffn_down_w, dtype=np.float32)

    wrvk = np.ascontiguousarray(
        W_rvk.reshape(3, 16, 128, 16, 128).transpose(1, 0, 4, 3, 2)
        .reshape(16, 3, 128, 2048))
    wout = np.ascontiguousarray(W_out.T.reshape(16, 128, 2048))
    wcls = np.ascontiguousarray(
        clsw.T.reshape(16, 128, 12).transpose(1, 0, 2).reshape(128, 192))
    gt7 = gate.transpose(0, 2, 1).reshape(8, 4, 4, 128, 4, 2, 128)
    ut7 = up.transpose(0, 2, 1).reshape(8, 4, 4, 128, 4, 2, 128)
    Gt = gt7.transpose(0, 4, 2, 3, 1, 5, 6)
    Ut = ut7.transpose(0, 4, 2, 3, 1, 5, 6)
    wgu = np.ascontiguousarray(
        np.stack([Gt, Ut], axis=6).reshape(8, 16, 128, 2048))
    wdn = np.ascontiguousarray(down.transpose(0, 2, 1).reshape(8, 8, 128, 2048))
    tdm = np.ascontiguousarray(tdv.reshape(16, 128).T)
    tfm = np.ascontiguousarray(tfv.reshape(16, 128).T)
    brep = np.ascontiguousarray(np.broadcast_to(biasv[None, :], (128, 12)))
    ident = np.eye(128, dtype=np.float32)

    nc = _get_nc()
    in_maps = []
    for k in range(8):
        t0 = 1024 * k
        if k == 0:
            xc = np.concatenate(
                [np.zeros((HALO, C), np.float32), x[0:TO + 0]], axis=0)
            mk = np.zeros((128, HALO), np.float32)
        else:
            xc = x[t0 - HALO:t0 + TO]
            mk = np.ones((128, HALO), np.float32)
        in_maps.append({
            "x": np.ascontiguousarray(xc), "wrvk": wrvk, "wout": wout,
            "wcls": wcls, "wgu": wgu, "wdn": wdn, "td": tdm, "tf": tfm,
            "brep": brep, "ident": ident, "mask": mk,
        })
    res = run_bass_kernel_spmd(nc, in_maps, core_ids=list(range(8)))
    out = np.concatenate([np.asarray(r["out"]) for r in res.results], axis=0)
    return out.reshape(1, 8192, 2048)
